# version 11
# baseline (speedup 1.0000x reference)
import os
os.environ.setdefault("TRN_ALLOW_ICI_ALL_TO_ALL", "1")
import math
import numpy as np
import ml_dtypes
import concourse.bass as bass
import concourse.mybir as mybir
from concourse.bass_utils import run_bass_kernel_spmd
from concourse.aot_env import allow_ici_all_to_all

F32 = mybir.dt.float32
BF16 = mybir.dt.bfloat16
AX = mybir.AxisListType.X
ALU = mybir.AluOpType
AF = mybir.ActivationFunctionType

NCORES = 8
B = 2
T = 4096
D = 4096
HD = 128
QH = 4
DC = D // 128
NT = B * T
TB = 256
NTB = NT // TB
TPB = NTB // B
OWN = 512
SM_SCALE = 1.0 / math.sqrt(HD)
SM_SHIFT = -30.0
NEG = -1.0e9
EPS = 1e-5
LORA_SCALE = 2.0
QMEAN_N = 3484.0
Q_START = 611
Q_END = 4095

ENGINES = ("sync", "tensor", "vector", "scalar", "gpsimd")


def _llama3_inv_freq(head_dim=128):
    base = 500000.0
    inv_freq = 1.0 / base ** (np.arange(0, head_dim, 2, dtype=np.float32) / head_dim)
    factor, low_f, high_f, old_ctx = 32.0, 1.0, 4.0, 8192
    low_wl, high_wl = old_ctx / low_f, old_ctx / high_f
    wavelen = 2 * math.pi / inv_freq
    inv_llama = np.where(wavelen > low_wl, inv_freq / factor, inv_freq)
    smooth = (old_ctx / wavelen - low_f) / (high_f - low_f)
    smoothed = (1 - smooth) * inv_llama / factor + smooth * inv_llama
    is_med = ~(wavelen < high_wl) & ~(wavelen > low_wl)
    return np.where(is_med, smoothed, inv_llama).astype(np.float32)


def _rope_tables():
    inv_freq = _llama3_inv_freq(HD)
    pos = np.arange(T, dtype=np.float32)
    freqs = pos[:, None] * inv_freq[None, :]
    emb = np.concatenate([freqs, freqs], axis=-1)
    cos = np.cos(emb).astype(np.float32)
    sin = np.sin(emb).astype(np.float32)
    return np.ascontiguousarray(cos.T), np.ascontiguousarray(sin.T)


class Sem:
    __slots__ = ("h", "name", "n")

    def __init__(self, h, name):
        self.h, self.name, self.n = h, name, 0


class Ctx:
    def __init__(self, nc):
        self.nc = nc
        self.cmds = {e: [] for e in ENGINES}
        self.last_wait = {e: {} for e in ENGINES}
        self._barn = 0

    def sem(self, name):
        return Sem(self.nc.alloc_semaphore(name), name)

    def _dedup(self, engine, waits):
        lw = self.last_wait[engine]
        ws = []
        for s, v in waits:
            if v is None or v <= 0:
                continue
            if lw.get(s.name, -1) < v:
                ws.append((s.h, v))
                lw[s.name] = v
        return ws

    def op(self, engine, fn, waits=(), inc=None, inck=1, predelay=0):
        ws = self._dedup(engine, waits)
        tok = None
        ih = None
        if inc is not None:
            inc.n += inck
            tok = inc.n
            ih = inc.h
        pd = predelay if ws else 0

        def emitter(eng, ws=ws, fn=fn, ih=ih, inck=inck, pd=pd):
            for h, v in ws:
                eng.wait_ge(h, v)
            if pd:
                eng.nop(cycle_cnt=pd)
            instr = fn(eng)
            if ih is not None:
                instr.then_inc(ih, inck)

        self.cmds[engine].append(emitter)
        return tok

    def wait(self, engine, *sem_vals):
        ws = self._dedup(engine, sem_vals)
        if not ws:
            return

        def emitter(eng, ws=ws):
            for h, v in ws:
                eng.wait_ge(h, v)

        self.cmds[engine].append(emitter)

    def barrier(self):
        self._barn += 1
        bar = self.sem(f"bar{self._barn}")
        for e in ENGINES:
            self.op(e, lambda eng: eng.nop(), inc=bar)
        tgt = bar.n
        for e in ENGINES:
            self.wait(e, (bar, tgt))


def build_nc(debug=False, phases=4):
    nc = bass.Bass("TRN2", target_bir_lowering=False)
    ctx = Ctx(nc)

    xT = nc.dram_tensor("xT", [D, NT], F32, kind="ExternalInput").ap()
    xownT = nc.dram_tensor("xownT", [D, 2 * OWN], F32, kind="ExternalInput").ap()
    wqkv_h = nc.dram_tensor("wqkv_h", [DC, 128, 6 * 128], F32, kind="ExternalInput").ap()
    woh = nc.dram_tensor("woh", [DC, 128, D], F32, kind="ExternalInput").ap()
    wh = nc.dram_tensor("wh", [DC, 128, D], F32, kind="ExternalInput").ap()
    ah = nc.dram_tensor("ah", [DC, 128, 128], F32, kind="ExternalInput").ap()
    blt_in = nc.dram_tensor("blt", [128, D], F32, kind="ExternalInput").ap()
    rwt_in = nc.dram_tensor("rwt", [128, DC * 8], F32, kind="ExternalInput").ap()
    rbias_in = nc.dram_tensor("rbias", [1, 8], F32, kind="ExternalInput").ap()
    btile_in = nc.dram_tensor("btile", [128, DC], F32, kind="ExternalInput").ap()
    rmask_in = nc.dram_tensor("rmask", [1, 2 * OWN], F32, kind="ExternalInput").ap()
    cosT = nc.dram_tensor("cosT", [128, T], F32, kind="ExternalInput").ap()
    sinT = nc.dram_tensor("sinT", [128, T], F32, kind="ExternalInput").ap()

    out_t = nc.dram_tensor("out_t", [D, 2 * OWN], F32, kind="ExternalOutput").ap()
    dbg = {}
    if debug:
        dbg["x1"] = nc.dram_tensor("dbg_x1", [128, DC * 2 * OWN], BF16, kind="ExternalOutput").ap()
        dbg["q"] = nc.dram_tensor("dbg_q", [QH * 128, NT], BF16, kind="ExternalOutput").ap()
        dbg["k"] = nc.dram_tensor("dbg_k", [128, NT], BF16, kind="ExternalOutput").ap()
        dbg["v"] = nc.dram_tensor("dbg_v", [128, NT], BF16, kind="ExternalOutput").ap()
        dbg["attn"] = nc.dram_tensor("dbg_attn", [2 * NCORES * 512, 512], BF16, kind="ExternalOutput").ap()
        dbg["routing"] = nc.dram_tensor("dbg_routing", [128, 2], F32, kind="ExternalOutput").ap()
        dbg["rsums"] = nc.dram_tensor("dbg_rsums", [128, 2 * DC], F32, kind="ExternalOutput").ap()
        dbg["rsumsg"] = nc.dram_tensor("dbg_rsumsg", [128, 2 * DC], F32, kind="ExternalOutput").ap()
        dbg["rsm"] = nc.dram_tensor("dbg_rsm", [8, 32], F32, kind="ExternalOutput").ap()

    q_dram = nc.dram_tensor("q_dram", [QH, 128, NT], BF16).ap()
    a2a_in = [nc.dram_tensor(f"a2a_in{b}", [NCORES * 512, 512], BF16).ap() for b in range(B)]
    a2a_out = [nc.dram_tensor(f"a2a_out{b}", [NCORES * 512, 512], BF16).ap() for b in range(B)]
    rt_in = nc.dram_tensor("rt_in", [128, 2 * DC], F32).ap()
    rt_out = nc.dram_tensor("rt_out", [128, 2 * DC], F32).ap()

    tri = np.where(np.arange(128)[None, :] >= np.arange(128)[:, None], 0.0, NEG)
    causal_c = nc.inline_tensor(tri.astype(ml_dtypes.bfloat16), "causal_c").ap()
    ident_bf_c = nc.inline_tensor(np.eye(128, dtype=ml_dtypes.bfloat16), "ident_bf_c").ap()
    ident_f_c = nc.inline_tensor(np.eye(128, dtype=np.float32), "ident_f_c").ap()
    ones_col_c = nc.inline_tensor(np.ones((128, 1), dtype=ml_dtypes.bfloat16), "ones_col_c").ap()
    ones_row_c = nc.inline_tensor(np.ones((1, 128), dtype=np.float32), "ones_row_c").ap()
    e2np = np.zeros((8, 128), dtype=np.float32)
    for e in range(8):
        e2np[e, e * 16:(e + 1) * 16] = LORA_SCALE
    e2_c = nc.inline_tensor(e2np, "e2_c").ap()

    AS = nc.alloc_sbuf_tensor
    causal = AS("causal", [128, 128], BF16).ap()
    identb = AS("identb", [128, 128], BF16).ap()
    identf = AS("identf", [128, 128], F32).ap()
    ones_col = AS("ones_col", [128, 1], BF16).ap()
    ones_row = AS("ones_row", [1, 128], F32).ap()
    e2_sb = AS("e2_sb", [8, 128], F32).ap()
    btile_sb = AS("btile_sb", [128, DC], F32).ap()
    rwt_sb = AS("rwt_sb", [128, DC * 8], F32).ap()
    rbias_sb = AS("rbias_sb", [1, 8], F32).ap()
    rmask_sb = AS("rmask_sb", [1, 2 * OWN], F32).ap()

    rsums = AS("rsums", [128, 2 * DC], F32).ap()
    rsums_g = AS("rsums_g", [128, 2 * DC], F32).ap()
    routing_sc = AS("routing_sc", [128, 2], F32).ap()
    rsm = AS("rsm", [8, 32], F32).ap()
    shift_sb = AS("shift_sb", [128, 1], F32).ap()
    eps_sb = AS("eps_sb", [128, 1], F32).ap()
    row_a = AS("row_a", [1, 512], F32).ap()
    row_b = AS("row_b", [1, 512], F32).ap()
    row_c = AS("row_c", [1, 512], F32).ap()
    stage = AS("stage", [128, 8192], F32).ap()

    AP_ = nc.alloc_psum_tensor
    pb = [AP_(f"pb{i}", [128, 512], F32).ap() for i in range(7)]
    pvt = AP_("pvt", [128, 1024], BF16).ap()

    g_kv = []
    def _alloc(name, shape, dt, bag):
        g = nc.sbuf_tensor(name, shape, dt)
        ap = g.__enter__().ap()
        bag.append(g)
        return ap

    wqkv_bf = _alloc("wqkv_bf", [128, DC * 768], BF16, g_kv)
    cos_bf = _alloc("cos_bf", [128, T], BF16, g_kv)
    sin_bf = _alloc("sin_bf", [128, T], BF16, g_kv)
    Karena = _alloc("Karena", [128, NT], BF16, g_kv)
    Vtm = _alloc("Vtm", [128, NT], BF16, g_kv)
    g_p1 = []
    x_bf = [_alloc("x_bf0", [128, DC * TB], BF16, g_p1),
            _alloc("x_bf1", [128, DC * TB], BF16, g_p1),
            _alloc("x_bf2", [128, DC * TB], BF16, g_p1)]
    qraw = _alloc("qraw", [128, 2 * TB], BF16, g_p1)
    qrope = _alloc("qrope", [128, 2 * TB], BF16, g_p1)
    rtmp = _alloc("rtmp", [128, TB], BF16, g_p1)
    vraw = _alloc("vraw", [128, 2 * TB], BF16, g_p1)

    def pad_tok(engine, sem, cycles=300):
        return ctx.op(engine, lambda e, c=cycles: e.nop(cycle_cnt=c), inc=sem)

    def dma(engine, out, in_, sem, waits=()):
        return ctx.op(engine, lambda e, o=out, i=in_: e.dma_start(out=o, in_=i),
                      waits=waits, inc=sem, inck=16)

    s_ldc = ctx.sem("s_ldc")
    s_ld0 = ctx.sem("s_ld0")
    s_ld1 = ctx.sem("s_ld1")
    s_cast = ctx.sem("s_cast")

    for (dst, src) in [(causal, causal_c), (identb, ident_bf_c), (identf, ident_f_c),
                       (ones_col, ones_col_c), (ones_row, ones_row_c), (e2_sb, e2_c),
                       (btile_sb, btile_in), (rwt_sb, rwt_in), (rbias_sb, rbias_in),
                       (rmask_sb, rmask_in)]:
        dma("sync", dst[:, :], src[:, :], s_ldc)

    t_cos = dma("sync", stage[:, 0:T], cosT[:, :], s_ld0)
    t_cc = ctx.op("vector", lambda e: e.tensor_copy(cos_bf[:, :], stage[:, 0:T]),
                  waits=[(s_ld0, t_cos)], inc=s_cast)
    t_sin = dma("sync", stage[:, 0:T], sinT[:, :], s_ld0, waits=[(s_cast, t_cc)])
    t_sc = ctx.op("vector", lambda e: e.tensor_copy(sin_bf[:, :], stage[:, 0:T]),
                  waits=[(s_ld0, t_sin)], inc=s_cast)
    ctx.op("vector", lambda e: e.memset(rsums[:, :], 0.0), inc=s_cast)
    ctx.op("vector", lambda e: e.memset(shift_sb[:, :], SM_SHIFT), inc=s_cast)
    ctx.op("vector", lambda e: e.memset(eps_sb[:, :], EPS), inc=s_cast)
    cs_done = s_cast.n

    half_free = {0: cs_done, 1: cs_done}
    for i in range(16):
        hf = i % 2
        st_v = stage[:, hf * 2048: hf * 2048 + 1536].rearrange("p (c f) -> p c f", c=2)
        st_f = stage[:, hf * 2048: hf * 2048 + 1536]
        t_d = dma("sync", st_v, wqkv_h[2 * i:2 * i + 2, :, :].rearrange("c p f -> p c f"),
                  (s_ld0 if hf == 0 else s_ld1), waits=[(s_cast, half_free[hf])])
        half_free[hf] = ctx.op(
            "vector",
            lambda e, o=wqkv_bf[:, i * 1536:(i + 1) * 1536], s=st_f: e.tensor_copy(o, s),
            waits=[((s_ld0 if hf == 0 else s_ld1), t_d)], inc=s_cast)
    wq_cast = s_cast.n

    ctx.wait("tensor", (s_ldc, s_ldc.n))
    ctx.wait("scalar", (s_ldc, s_ldc.n))
    ctx.wait("vector", (s_ldc, s_ldc.n))
    ctx.barrier()

    s_xd = [ctx.sem(f"s_xd{i}") for i in range(4)]
    s_xc = ctx.sem("s_xc")
    s_pe1 = ctx.sem("s_pe1")
    s_ac1 = ctx.sem("s_ac1")
    s_rp1 = ctx.sem("s_rp1")
    s_qd = [ctx.sem("s_qd0"), ctx.sem("s_qd1")]

    xT_v = xT.rearrange("(c p) t -> p c t", p=128)

    xdma_tok, xcast_tok, xfree_tok = {}, {}, {}
    evac_tok, rope_tok, qdma_tok = {}, {}, {}
    vt_tok = {}
    evac_tok_psum = {}

    def prev_qraw(tb, of):
        chain = [0, 2, 4] if of % 2 == 0 else [1, 3]
        i = chain.index(of)
        if i > 0:
            return (tb, chain[i - 1])
        return (tb - 1, chain[-1]) if tb > 0 else None

    def prev_qrope(tb, of):
        chain = [0, 2] if of % 2 == 0 else [1, 3]
        i = chain.index(of)
        if i > 0:
            return (tb, chain[i - 1])
        return (tb - 1, chain[-1]) if tb > 0 else None

    def emit_vtrans(tprev):
        vsl = vraw[:, (tprev % 2) * TB:(tprev % 2 + 1) * TB]
        pslot = (tprev % 2) * 512
        wv = [(s_ac1, evac_tok[(tprev, 5)])]
        if tprev >= 2:
            wv.append((s_ac1, vt_tok[("ac", tprev - 2)]))
        tok = None
        for i in range(TB // 128):
            tok = ctx.op(
                "tensor",
                lambda e, o=pvt[:, pslot + i * 128: pslot + (i + 1) * 128],
                       s=vsl[:, i * 128:(i + 1) * 128]: e.transpose(o, s, identb[:, :]),
                waits=(wv if i == 0 else ()), inc=s_pe1, predelay=(200 if i == 0 else 0))
        vt_tok[("pe", tprev)] = tok
        vt_tok[("ac", tprev)] = ctx.op(
            "scalar",
            lambda e, o=Vtm[:, tprev * TB:(tprev + 1) * TB],
                   s=pvt[:, pslot:pslot + TB]: e.copy(o, s),
            waits=[(s_pe1, tok)], inc=s_ac1)

    for tb in range(NTB):
        par = tb % 3
        for hf in range(4):
            st_v = stage[:, hf * 2048:(hf + 1) * 2048]
            waits = []
            if tb == 0:
                waits.append((s_cast, wq_cast))
            if tb >= 1:
                waits.append((s_xc, xcast_tok[(tb - 1, hf)]))
            if tb >= 3:
                waits.append((s_pe1, xfree_tok[tb - 3]))
            xdma_tok[(tb, hf)] = dma(
                "sync",
                st_v.rearrange("p (c t) -> p c t", c=8),
                xT_v[:, 8 * hf:8 * hf + 8, tb * TB:(tb + 1) * TB],
                s_xd[hf], waits=waits)
            xcast_tok[(tb, hf)] = ctx.op(
                "vector",
                lambda e, o=x_bf[par][:, hf * 8 * TB:(hf + 1) * 8 * TB], s=st_v:
                    e.tensor_copy(o, s),
                waits=[(s_xd[hf], xdma_tok[(tb, hf)])], inc=s_xc)

        for of in range(6):
            gidx = tb * 6 + of
            slot = gidx % 3
            ps = pb[slot]
            w0 = [(s_xc, xcast_tok[(tb, 3)]), (s_cast, wq_cast)]
            if gidx >= 3:
                w0.append((s_ac1, evac_tok_psum[gidx - 3]))
            mm_tok = None
            for dc in range(DC):
                mm_tok = ctx.op(
                    "tensor",
                    lambda e, o=ps[:, 0:TB],
                           l=wqkv_bf[:, dc * 768 + of * 128: dc * 768 + (of + 1) * 128],
                           r=x_bf[par][:, dc * TB:(dc + 1) * TB],
                           st=(dc == 0), sp=(dc == DC - 1): e.matmul(o, l, r, start=st, stop=sp),
                    waits=(w0 if dc == 0 else ()),
                    inc=(s_pe1 if dc == DC - 1 else None))
            if of == 5:
                xfree_tok[tb] = mm_tok

            if of <= 4:
                qpar = of % 2
                if of < 4:
                    rawsl = qraw[:, qpar * TB:(qpar + 1) * TB]
                    ropesl = qrope[:, qpar * TB:(qpar + 1) * TB]
                else:
                    rawsl = qraw[:, 0:TB]
                    ropesl = Karena[:, tb * TB:(tb + 1) * TB]
                w2 = [(s_pe1, mm_tok)]
                pq = prev_qraw(tb, of)
                if pq is not None and pq in rope_tok:
                    w2.append((s_rp1, rope_tok[pq]))
                evac_tok_psum[gidx] = evac_tok[(tb, of)] = ctx.op(
                    "scalar", lambda e, o=rawsl, s=ps[:, 0:TB]: e.copy(o, s),
                    waits=w2, inc=s_ac1, predelay=150)
                pos0 = (tb % TPB) * TB
                cosv = cos_bf[:, pos0:pos0 + TB]
                sinv = sin_bf[:, pos0:pos0 + TB]
                wr = [(s_ac1, evac_tok[(tb, of)])]
                if of < 4:
                    pr = prev_qrope(tb, of)
                    if pr is not None and pr in qdma_tok:
                        wr.append((s_qd[of % 2], qdma_tok[pr]))
                ctx.op("vector",
                       lambda e, o=rtmp[0:64, :], i=rawsl[64:128, :]:
                           e.tensor_scalar_mul(o, i, -1.0), waits=wr, predelay=200)
                ctx.op("vector",
                       lambda e, o=rtmp[64:128, :], i=rawsl[0:64, :]: e.tensor_copy(o, i))
                ctx.op("vector",
                       lambda e, o=ropesl, i=rawsl, c=cosv: e.tensor_mul(o, i, c))
                ctx.op("vector",
                       lambda e, s=sinv: e.tensor_mul(rtmp[:, :], rtmp[:, :], s))
                rope_tok[(tb, of)] = ctx.op(
                    "vector", lambda e, o=ropesl: e.tensor_add(o, o, rtmp[:, :]),
                    inc=s_rp1)
                if of < 4:
                    qdma_tok[(tb, of)] = dma(
                        "gpsimd", q_dram[of, :, tb * TB:(tb + 1) * TB], ropesl,
                        s_qd[of % 2], waits=[(s_rp1, rope_tok[(tb, of)])])
            else:
                vsl = vraw[:, (tb % 2) * TB:(tb % 2 + 1) * TB]
                w2 = [(s_pe1, mm_tok)]
                if tb >= 2:
                    w2.append((s_pe1, vt_tok[("pe", tb - 2)]))
                evac_tok_psum[gidx] = evac_tok[(tb, 5)] = ctx.op(
                    "scalar", lambda e, o=vsl, s=ps[:, 0:TB]: e.copy(o, s),
                    waits=w2, inc=s_ac1, predelay=150)
        if tb >= 1:
            emit_vtrans(tb - 1)
    emit_vtrans(NTB - 1)

    ctx.barrier()

    def emit_dumps_and_finish(s_fin):
        dma("gpsimd", dbg["k"][:, :], Karena[:, :], s_fin)
        dma("gpsimd", dbg["v"][:, :], Vtm[:, :], s_fin)
        dma("gpsimd", dbg["q"].rearrange("(h p) t -> h p t", p=128), q_dram[:, :, :], s_fin)
        ctx.wait("gpsimd", (s_fin, s_fin.n))
        with allow_ici_all_to_all():
            with nc.Block() as block:
                block.sync(lambda e: [f(e) for f in ctx.cmds["sync"]])
                block.tensor(lambda e: [f(e) for f in ctx.cmds["tensor"]])
                block.vector(lambda e: [f(e) for f in ctx.cmds["vector"]])
                block.scalar(lambda e: [f(e) for f in ctx.cmds["scalar"]])
                block.gpsimd(lambda e: [f(e) for f in ctx.cmds["gpsimd"]])
        return nc

    if phases == 1:
        return emit_dumps_and_finish(ctx.sem("s_fin"))

    for g in reversed(g_p1):
        g.__exit__(None, None, None)
    g_p1 = []
    g_p2 = []
    qu = [_alloc("qu0", [128, T], BF16, g_p2), _alloc("qu1", [128, T], BF16, g_p2)]
    pt = _alloc("pt", [128, 4 * 512], BF16, g_p2)
    a2at = _alloc("a2at", [128, 2 * 512], BF16, g_p2)
    rbc = _alloc("rbc", [128, 512], F32, g_p2)
    recipr = _alloc("recipr", [1, 512], F32, g_p2)

    s_qld = [ctx.sem("s_qld0"), ctx.sem("s_qld1")]
    s_st = ctx.sem("s_st")
    s_msk = ctx.sem("s_msk")
    s_exp = ctx.sem("s_exp")
    s_av = ctx.sem("s_av")
    s_nrm = ctx.sem("s_nrm")
    s_bcst = ctx.sem("s_bcst")
    s_bcc = ctx.sem("s_bcc")
    s_a2d = [ctx.sem("s_a2d0"), ctx.sem("s_a2d1")]
    s_rtd = ctx.sem("s_rtd")
    s_cc = ctx.sem("s_cc")

    st_tok, exp_tok, av_tok = {}, {}, {}
    unit_last_st = {}
    slot_bc_guard = {}
    avb_free = {}
    sum_free = 0
    kchunk = 0
    nblk = 0

    for u in range(2 * QH):
        b, h = u // QH, u % QH
        par = u % 2
        wq_ = [(s_qd[0], s_qd[0].n), (s_qd[1], s_qd[1].n)]
        if u < 2:
            wq_ += [(s_pe1, s_pe1.n), (s_rp1, s_rp1.n)]
        if u >= 2:
            wq_.append((s_st, unit_last_st[u - 2]))
        t_q = dma("sync", qu[par][:, :], q_dram[h, :, b * T:(b + 1) * T],
                  s_qld[par], waits=wq_)
        descs = []
        for qb in range(8):
            nch = 4 * qb + 4
            for c in range(nch):
                sufs = 0 if c < 4 * qb else (c - 4 * qb) * 128
                descs.append((qb, c, sufs, c >= 4 * qb, c == nch - 1))
        N = len(descs)
        LA = 3
        k0 = kchunk

        def emit_st(i):
            qb, c, sufs, diag, last = descs[i]
            kk = k0 + i
            slot = kk % 4
            wst = [(s_qld[par], t_q)]
            if kk >= 4:
                wst.append((s_exp, exp_tok[kk - 4]))
            if slot_bc_guard.get(slot):
                wst.append((s_bcc, slot_bc_guard.pop(slot)))
            ctx.op(
                "tensor",
                lambda e, o=pb[slot][:, sufs:512],
                       l=Karena[:, b * T + c * 128: b * T + (c + 1) * 128],
                       r=qu[par][:, qb * 512 + sufs:(qb + 1) * 512],
                       sp=(not diag): e.matmul(o, l, r, start=True, stop=sp),
                waits=wst, inc=(None if diag else s_st))
            if diag:
                ctx.op(
                    "tensor",
                    lambda e, o=pb[slot][:, sufs:sufs + 128]:
                        e.matmul(o, identb[:, :], causal[:, :], start=False, stop=True),
                    inc=s_st)
            st_tok[kk] = s_st.n
            unit_last_st[u] = s_st.n

        for j in range(min(LA, N)):
            emit_st(j)
        for i in range(N):
            qb, c, sufs, diag, last = descs[i]
            kk = k0 + i
            slot = kk % 4
            nch = 4 * qb + 4
            avb = pb[4 + nblk % 2]
            we = [(s_st, st_tok[kk])]
            if kk >= 4:
                we.append((s_av, av_tok[kk - 4]))
            exp_tok[kk] = ctx.op(
                "scalar",
                lambda e, o=pt[:, slot * 512 + sufs:(slot + 1) * 512],
                       s=pb[slot][:, sufs:512]:
                    e.activation(o, s, AF.Exp, bias=shift_sb[:, 0:1], scale=SM_SCALE),
                waits=we, inc=s_exp, predelay=(150 if diag else 0))
            wav = [(s_exp, exp_tok[kk])]
            if c == 0:
                wav.append((s_nrm, avb_free.get(nblk % 2, 0)))
            ctx.op(
                "tensor",
                lambda e, o=avb[:, sufs:512],
                       l=Vtm[:, (b * 32 + c) * 128:(b * 32 + c + 1) * 128],
                       r=pt[:, slot * 512 + sufs:(slot + 1) * 512],
                       st=(c == 0), sp=(c == nch - 1): e.matmul(o, l, r, start=st, stop=sp),
                waits=wav)
            wsum = []
            if c == 0:
                wsum.append((s_nrm, sum_free))
            av_tok[kk] = ctx.op(
                "tensor",
                lambda e, o=pb[6][0:1, sufs:512],
                       r=pt[:, slot * 512 + sufs:(slot + 1) * 512],
                       st=(c == 0), sp=(c == nch - 1):
                    e.matmul(o, ones_col[:, :], r, start=st, stop=sp),
                waits=wsum, inc=s_av)
            if i + LA < N:
                emit_st(i + LA)
            if last:
                rc_tok = ctx.op(
                    "vector", lambda e: e.reciprocal(recipr[:, :], pb[6][0:1, :]),
                    waits=[(s_av, av_tok[kk]), (s_bcst, s_bcst.n)], inc=s_nrm,
                    predelay=150)
                sum_free = rc_tok
                bslot = slot
                bc_tok = ctx.op(
                    "tensor",
                    lambda e, o=pb[bslot][:, :]: e.matmul(o, ones_row[0:1, :], recipr[0:1, :],
                                                          start=True, stop=True),
                    waits=[(s_nrm, rc_tok), (s_exp, exp_tok[kk]), (s_bcc, s_bcc.n)],
                    inc=s_bcst, predelay=300)
                slot_bc_guard[bslot] = None
                wbcc = [(s_bcst, bc_tok)]
                if nblk >= 1:
                    wbcc.append((s_nrm, s_nrm.n))
                bcc_tok = ctx.op(
                    "scalar", lambda e, o=pb[bslot]: e.copy(rbc[:, :], o[:, :]),
                    waits=wbcc, inc=s_bcc)
                slot_bc_guard[bslot] = bcc_tok
                asl = a2at[:, (nblk % 2) * 512:(nblk % 2 + 1) * 512]
                wn = [(s_bcc, bcc_tok)]
                if nblk >= 2:
                    wn.append((s_a2d[nblk % 2], 16 * ((nblk - 2) // 2 + 1)))
                n_tok = ctx.op(
                    "vector", lambda e, o=asl, a=avb: e.tensor_mul(o, a[:, :], rbc[:, :]),
                    waits=wn, inc=s_nrm, predelay=300)
                avb_free[nblk % 2] = n_tok
                dma("gpsimd", a2a_in[b][qb * 512 + h * 128: qb * 512 + (h + 1) * 128, :], asl,
                    s_a2d[nblk % 2], waits=[(s_nrm, n_tok)])
                nblk += 1
        kchunk += N
        if h == QH - 1:
            ctx.wait("gpsimd", (s_a2d[0], s_a2d[0].n), (s_a2d[1], s_a2d[1].n))
            ctx.op("gpsimd",
                   lambda e, i=a2a_in[b], o=a2a_out[b]: e.collective_compute(
                       "AllToAll", ALU.bypass,
                       replica_groups=[list(range(NCORES))],
                       ins=[i.opt()], outs=[o.opt()]),
                   inc=s_cc)

    ctx.barrier()
    if debug:
        s_dbg = ctx.sem("s_dbg")
        dma("gpsimd", dbg["k"][:, :], Karena[:, :], s_dbg)
        dma("gpsimd", dbg["v"][:, :], Vtm[:, :], s_dbg)
        dma("gpsimd", dbg["q"].rearrange("(h p) t -> h p t", p=128), q_dram[:, :, :], s_dbg)
        for b_ in range(B):
            dma("gpsimd", dbg["attn"][b_ * NCORES * 512:(b_ + 1) * NCORES * 512, :],
                a2a_out[b_][:, :], s_dbg, waits=[(s_cc, b_ + 1)])
        ctx.wait("gpsimd", (s_dbg, s_dbg.n))
        ctx.barrier()
    for g in reversed(g_p2):
        g.__exit__(None, None, None)
    for g in reversed(g_kv):
        g.__exit__(None, None, None)

    g_p3 = []
    x1 = _alloc("x1", [128, DC * 2 * OWN], BF16, g_p3)
    attn_all = _alloc("attn_all", [128, DC * 512], BF16, g_p3)
    w_bf = [_alloc("w_bf0", [128, 4096], BF16, g_p3), _alloc("w_bf1", [128, 4096], BF16, g_p3),
            _alloc("w_bf2", [128, 4096], BF16, g_p3)]
    resid = _alloc("resid", [128, 2 * 512], F32, g_p3)
    sqtile = _alloc("sqtile", [128, 2 * 512], BF16, g_p3)
    ytmpf = _alloc("ytmpf", [128, 512], F32, g_p3)
    bcab = _alloc("bcab", [128, 2 * 512], F32, g_p3)
    mask_bc = _alloc("mask_bc", [128, 512], F32, g_p3)
    t_bf = _alloc("t_bf", [128, 2 * 512], BF16, g_p3)
    at_bf = _alloc("at_bf", [128, DC * 128], BF16, g_p3)
    blt_bf = _alloc("blt_bf", [128, D], BF16, g_p3)
    osl = _alloc("osl", [128, 2 * 512], F32, g_p3)

    s_a2r = ctx.sem("s_a2r")
    s_wd3 = [ctx.sem(f"s_wd3{i}") for i in range(4)]
    s_wc3 = ctx.sem("s_wc3")
    s_pe3 = ctx.sem("s_pe3")
    s_y3 = ctx.sem("s_y3")
    s_ac3 = ctx.sem("s_ac3")
    s_rd3 = [ctx.sem("s_rd30"), ctx.sem("s_rd31")]
    s_od = [ctx.sem("s_od0"), ctx.sem("s_od1")]

    def fetch_w(src3d, oc, gi, wdma_tok, wcast_tok, mm_done_tok):
        wpar = gi % 3
        for hf in range(2):
            sidx = (gi * 2 + hf) % 4
            st_f = stage[:, sidx * 2048:(sidx + 1) * 2048]
            wd = []
            if (gi - 2, hf) in wcast_tok:
                wd.append((s_wc3, wcast_tok[(gi - 2, hf)]))
            wdma_tok[(gi, hf)] = dma("sync", st_f, src3d[oc, :, hf * 2048:(hf + 1) * 2048],
                                     s_wd3[sidx], waits=wd)
            wc = [(s_wd3[sidx], wdma_tok[(gi, hf)])]
            if hf == 0 and (gi - 3) in mm_done_tok:
                wc.append((s_pe3, mm_done_tok[gi - 3]))
            wcast_tok[(gi, hf)] = ctx.op(
                "vector",
                lambda e, o=w_bf[wpar][:, hf * 2048:(hf + 1) * 2048], s=st_f: e.tensor_copy(o, s),
                waits=wc, inc=s_wc3)
        wcast_tok[gi] = wcast_tok[(gi, 1)]

    def emit_ln(blk, bc_wait_tok, out_final, ysl_fn=None):
        sq_tok = {}
        stats_tok = {}
        if ysl_fn is None:
            ysl_fn = lambda oc: x1[:, oc * 1024 + blk * 512: oc * 1024 + (blk + 1) * 512]
        for oc in range(DC):
            ysl = ysl_fn(oc)
            wsq = [(s_y3, yready[(blk, oc)])]
            if oc >= 2:
                wsq.append((s_pe3, stats_tok[oc - 2]))
            sq_tok[oc] = ctx.op(
                "scalar",
                lambda e, o=sqtile[:, (oc % 2) * 512:(oc % 2 + 1) * 512], s=ysl:
                    e.activation(o, s, AF.Square),
                waits=wsq, inc=s_ac3, predelay=150)
            wst = [(s_y3, yready[(blk, oc)])]
            if oc == 0:
                wst.append((s_y3, bc_wait_tok))
            ctx.op("tensor",
                   lambda e, o=pb[2][0:1, :], r=ysl, st=(oc == 0), sp=(oc == DC - 1):
                       e.matmul(o, ones_col[:, :], r, start=st, stop=sp),
                   waits=wst)
            stats_tok[oc] = ctx.op(
                "tensor",
                lambda e, o=pb[3][0:1, :], r=sqtile[:, (oc % 2) * 512:(oc % 2 + 1) * 512],
                       st=(oc == 0), sp=(oc == DC - 1):
                    e.matmul(o, ones_col[:, :], r, start=st, stop=sp),
                waits=[(s_ac3, sq_tok[oc])], inc=s_pe3, predelay=150)
        t1 = ctx.op("vector", lambda e: e.tensor_scalar_mul(row_a[:, :], pb[2][0:1, :], 1.0 / D),
                    waits=[(s_pe3, stats_tok[DC - 1])], inc=s_y3, predelay=150)
        ctx.op("vector", lambda e: e.tensor_scalar_mul(row_b[:, :], pb[3][0:1, :], 1.0 / D))
        ctx.op("vector", lambda e: e.tensor_mul(row_c[:, :], row_a[:, :], row_a[:, :]))
        t2 = ctx.op("vector", lambda e: e.tensor_sub(row_b[:, :], row_b[:, :], row_c[:, :]),
                    inc=s_y3)
        ctx.op("scalar", lambda e: e.activation(row_c[:, :], row_b[:, :], AF.Ln, bias=eps_sb[0:1, 0:1]),
               waits=[(s_y3, t2)], inc=s_ac3, predelay=300)
        t4 = ctx.op("scalar", lambda e: e.activation(row_b[:, :], row_c[:, :], AF.Exp, scale=-0.5),
                    inc=s_ac3)
        t5 = ctx.op("vector",
                    lambda e: e.scalar_tensor_tensor(row_c[:, :], row_a[:, :], -1.0, row_b[:, :],
                                                     ALU.mult, ALU.mult),
                    waits=[(s_ac3, t4)], inc=s_y3, predelay=300)
        tba = ctx.op("tensor", lambda e: e.matmul(pb[4][:, :], ones_row[0:1, :], row_b[0:1, :],
                                                  start=True, stop=True),
                     waits=[(s_y3, t5), (s_ac3, s_ac3.n), (s_y3, s_y3.n)], inc=s_pe3,
                     predelay=300)
        tbc = ctx.op("tensor", lambda e: e.matmul(pb[5][:, :], ones_row[0:1, :], row_c[0:1, :],
                                                  start=True, stop=True), inc=s_pe3)
        tca = ctx.op("scalar", lambda e: e.copy(bcab[:, 0:512], pb[4][:, :]),
                     waits=[(s_pe3, tbc)], inc=s_ac3)
        tcc = ctx.op("scalar", lambda e: e.copy(bcab[:, 512:1024], pb[5][:, :]), inc=s_ac3)
        norm_toks = {}
        for oc in range(DC):
            ysl = ysl_fn(oc)
            if out_final:
                odst = osl[:, (oc % 2) * 512:(oc % 2 + 1) * 512]
                wn = [(s_ac3, tcc)]
                if oc >= 2:
                    wn.append((s_od[oc % 2], od_tok[(blk, oc - 2)]))
                elif blk >= 1:
                    wn.append((s_od[oc % 2], od_tok[(blk - 1, DC - 2 + oc)]))
                ctx.op("vector", lambda e, y=ysl, o=ytmpf: e.tensor_mul(o[:, :], y, bcab[:, 0:512]),
                       waits=wn, predelay=300)
                norm_toks[oc] = ctx.op(
                    "vector", lambda e, o=odst: e.tensor_add(o, ytmpf[:, :], bcab[:, 512:1024]),
                    inc=s_y3)
                od_tok[(blk, oc)] = dma(
                    "gpsimd", out_t[oc * 128:(oc + 1) * 128, blk * 512:(blk + 1) * 512],
                    odst, s_od[oc % 2], waits=[(s_y3, norm_toks[oc])])
            else:
                ctx.op("vector", lambda e, y=ysl: e.tensor_mul(ytmpf[:, :], y, bcab[:, 0:512]),
                       waits=[(s_ac3, tcc)], predelay=300)
                norm_toks[oc] = ctx.op(
                    "vector", lambda e, y=ysl: e.tensor_add(y, ytmpf[:, :], bcab[:, 512:1024]),
                    inc=s_y3)
        return tca, tcc, norm_toks

    yready = {}
    od_tok = {}
    wo_mm_tok = {}
    wo_y_tok = {}
    wdma3, wcast3 = {}, {}
    x1row_free = 0
    stt_done = 0
    for blk in range(2):
        ctx.wait("sync", (s_cc, blk + 1))
        for s in range(NCORES):
            wr = [(s_pe3, wo_mm_tok[DC - 1])] if (blk == 1 and s == 0) else []
            rb_tok = dma("sync",
                         attn_all[:, s * 2048:(s + 1) * 2048].rearrange(
                             "p (h t) -> p h t", h=4),
                         a2a_out[blk][s * 512:(s + 1) * 512, :].rearrange(
                             "(h p) t -> p h t", p=128),
                         s_a2r, waits=wr)
        if blk == 0:
            for hf_ in range(2):
                wcast3[(-2, hf_)] = None
                wcast3[(-1, hf_)] = None
            ctx.wait("sync", (s_xc, s_xc.n))
            fetch_w(woh, 0, 0, wdma3, wcast3, wo_mm_tok)
            fetch_w(woh, 1, 1, wdma3, wcast3, wo_mm_tok)
        for oc in range(DC):
            gi = blk * DC + oc
            rwt_ = []
            if gi >= 2:
                rwt_.append((s_y3, yready[(blk, oc - 2)] if oc >= 2 else yready[(blk - 1, DC - 2 + oc)]))
            rd_tok = dma("gpsimd", resid[:, (gi % 2) * 512:(gi % 2 + 1) * 512],
                         xownT[oc * 128:(oc + 1) * 128, blk * 512:(blk + 1) * 512],
                         s_rd3[gi % 2], waits=rwt_)
            wmm = [(s_a2r, rb_tok), (s_wc3, wcast3[gi])]
            if gi >= 2:
                wmm.append((s_y3, wo_y_tok[gi - 2]))
            tok = None
            for fc in range(DC):
                tok = ctx.op(
                    "tensor",
                    lambda e, o=pb[gi % 2][:, :],
                           l=w_bf[gi % 3][:, fc * 128:(fc + 1) * 128],
                           r=attn_all[:, fc * 512:(fc + 1) * 512],
                           st=(fc == 0), sp=(fc == DC - 1): e.matmul(o, l, r, start=st, stop=sp),
                    waits=(wmm if fc == 0 else ()),
                    inc=(s_pe3 if fc == DC - 1 else None))
            wo_mm_tok[gi] = tok
            wo_y_tok[gi] = yready[(blk, oc)] = ctx.op(
                "vector",
                lambda e, o=x1[:, oc * 1024 + blk * 512: oc * 1024 + (blk + 1) * 512],
                       p=pb[gi % 2][:, :], r=resid[:, (gi % 2) * 512:(gi % 2 + 1) * 512]:
                    e.tensor_add(o, p, r),
                waits=[(s_pe3, tok), (s_rd3[gi % 2], rd_tok)], inc=s_y3, predelay=150)
            if gi + 2 < 2 * DC:
                fetch_w(woh, (gi + 2) % DC, gi + 2, wdma3, wcast3, wo_mm_tok)
        tca, tcc, norm_toks = emit_ln(blk, x1row_free, out_final=False)
        x1row_free = s_y3.n
        tmb = ctx.op("tensor",
                     lambda e, r=rmask_sb[0:1, blk * 512:(blk + 1) * 512]:
                         e.matmul(pb[4][:, :], ones_row[0:1, :], r, start=True, stop=True),
                     waits=[(s_ac3, tca), (s_y3, stt_done)], inc=s_pe3)
        tmc = ctx.op("scalar",
                     lambda e: e.copy(mask_bc[:, :], pb[4][:, :]),
                     waits=[(s_pe3, tmb)], inc=s_ac3)
        for oc in range(DC):
            ysl = x1[:, oc * 1024 + blk * 512: oc * 1024 + (blk + 1) * 512]
            ctx.op("vector",
                   lambda e, y=ysl, o=rsums[:, blk * DC + oc: blk * DC + oc + 1],
                          m=mask_bc[:, :]:
                       e.scalar_tensor_tensor(ytmpf[:, :], y, 1.0, m, ALU.mult, ALU.mult,
                                              accum_out=o),
                   waits=[(s_y3, norm_toks[oc]), (s_ac3, tmc)],
                   predelay=300,
                   inc=(s_y3 if oc == DC - 1 else None))
        stt_done = s_y3.n
    rsums_done = s_y3.n

    t_ri = dma("gpsimd", rt_in[:, :], rsums[:, :], s_rtd, waits=[(s_y3, rsums_done)])
    ctx.wait("gpsimd", (s_rtd, t_ri))
    ctx.op("gpsimd",
           lambda e: e.collective_compute(
               "AllReduce", ALU.add, replica_groups=[list(range(NCORES))],
               ins=[rt_in.opt()], outs=[rt_out.opt()]),
           inc=s_cc)
    if debug:
        t_dx = dma("gpsimd", dbg["x1"][:, :], x1[:, :], s_od[0], waits=[(s_y3, rsums_done)])
        ctx.wait("gpsimd", (s_od[0], t_dx))

    ctx.barrier()

    s_pe4 = s_pe3
    s_y4 = s_y3
    s_ac4 = s_ac3

    t_rg = dma("gpsimd", rsums_g[:, :], rt_out[:, :], s_a2r, waits=[(s_cc, 3)])
    ctx.wait("sync", (s_wc3, s_wc3.n))
    ab_cast = {}
    for i in range(4):
        st_v = stage[:, i * 2048: i * 2048 + 1024].rearrange("p (c f) -> p c f", c=8)
        st_f = stage[:, i * 2048: i * 2048 + 1024]
        t_d = dma("sync", st_v, ah[8 * i:8 * i + 8, :, :].rearrange("c p f -> p c f"),
                  s_wd3[i])
        ab_cast[i] = ctx.op(
            "vector",
            lambda e, o=at_bf[:, i * 1024:(i + 1) * 1024], s=st_f: e.tensor_copy(o, s),
            waits=[(s_wd3[i], t_d)], inc=s_wc3)
    for i in range(4):
        st_f = stage[:, i * 2048 + 1024: (i + 1) * 2048]
        t_d = dma("sync", st_f, blt_in[:, i * 1024:(i + 1) * 1024], s_wd3[i])
        ab_cast[4 + i] = ctx.op(
            "vector",
            lambda e, o=blt_bf[:, i * 1024:(i + 1) * 1024], s=st_f: e.tensor_copy(o, s),
            waits=[(s_wd3[i], t_d)], inc=s_wc3)
    abprep = s_wc3.n
    t_mm_tok = {}
    for blk in range(2):
        tok = None
        for dc in range(DC):
            tok = ctx.op(
                "tensor",
                lambda e, o=pb[2 + blk][:, :],
                       l=at_bf[:, dc * 128:(dc + 1) * 128],
                       r=x1[:, dc * 1024 + blk * 512: dc * 1024 + (blk + 1) * 512],
                       st=(dc == 0), sp=(dc == DC - 1): e.matmul(o, l, r, start=st, stop=sp),
                waits=([(s_wc3, abprep)] if dc == 0 else ()),
                inc=(s_pe4 if dc == DC - 1 else None))
        t_mm_tok[blk] = tok
    rg_v = rsums_g.rearrange("p (b c) -> p b c", b=2)
    ltok = None
    for dc in range(DC):
        ltok = ctx.op(
            "tensor",
            lambda e, l=rg_v[:, :, dc], r=rwt_sb[:, dc * 8:(dc + 1) * 8]:
                e.matmul(pb[5][0:2, 0:8], l, r, start=(dc == 0), stop=False),
            waits=([(s_a2r, t_rg)] if dc == 0 else ()))
    ltok = ctx.op("tensor",
                  lambda e: e.matmul(pb[5][0:2, 0:8], ones_row[0:1, 0:2], rbias_sb[0:1, 0:8],
                                     start=False, stop=True),
                  inc=s_pe4)
    rl = ctx.op("scalar", lambda e: e.copy(rsm[0:2, 8:16], pb[5][0:2, 0:8]),
                waits=[(s_pe4, ltok)], inc=s_ac4)
    r1b = ctx.op("vector",
                 lambda e: e.reduce_max(rsm[0:2, 1:2], pb[5][0:2, 0:8], axis=AX, negate=True),
                 waits=[(s_pe4, ltok)], inc=s_y4)
    r3 = ctx.op("scalar",
                lambda e: e.activation(rsm[0:2, 16:24], pb[5][0:2, 0:8], AF.Exp,
                                       bias=rsm[0:2, 1:2]),
                waits=[(s_y4, r1b), (s_ac4, rl)], inc=s_ac4, predelay=600)
    ctx.op("vector", lambda e: e.reduce_sum(rsm[0:2, 2:3], rsm[0:2, 16:24], axis=AX),
           waits=[(s_ac4, r3)], predelay=600)
    ctx.op("vector", lambda e: e.nop(cycle_cnt=1000))
    ctx.op("vector", lambda e: e.reciprocal(rsm[0:2, 3:4], rsm[0:2, 2:3]))
    ctx.op("vector", lambda e: e.nop(cycle_cnt=1000))
    r5 = ctx.op("vector",
                lambda e: e.tensor_scalar_mul(rsm[0:2, 4:12], rsm[0:2, 16:24], rsm[0:2, 3:4]),
                inc=s_y4)
    r6 = ctx.op("tensor",
                lambda e: e.transpose(pb[6][0:8, 0:2], rsm[0:2, 4:12], identf[0:2, 0:2]),
                waits=[(s_y4, r5)], inc=s_pe4, predelay=600)
    r7 = ctx.op("scalar", lambda e: e.copy(rsm[0:8, 12:14], pb[6][0:8, 0:2]),
                waits=[(s_pe4, r6)], inc=s_ac4)
    r8 = ctx.op("tensor",
                lambda e: e.matmul(pb[5][:, 0:2], e2_sb[0:8, :], rsm[0:8, 12:14],
                                   start=True, stop=True),
                waits=[(s_ac4, r7)], inc=s_pe4, predelay=600)
    r9 = ctx.op("scalar", lambda e: e.copy(routing_sc[:, :], pb[5][:, 0:2]),
                waits=[(s_pe4, r8)], inc=s_ac4)
    if debug:
        dma("gpsimd", dbg["routing"][:, :], routing_sc[:, :], s_od[0],
            waits=[(s_ac4, r9)])
        dma("gpsimd", dbg["rsums"][:, :], rsums[:, :], s_od[0], waits=[(s_y3, rsums_done)])
        dma("gpsimd", dbg["rsumsg"][:, :], rsums_g[:, :], s_od[0], waits=[(s_a2r, t_rg)])
        dma("gpsimd", dbg["rsm"][:, :], rsm[:, :], s_od[0], waits=[(s_ac4, r9)])
    ts_tok = {}
    for blk in range(2):
        ts_tok[blk] = ctx.op(
            "vector",
            lambda e, o=t_bf[:, blk * 512:(blk + 1) * 512], p=pb[2 + blk],
                   sc=routing_sc[:, blk:blk + 1]:
                e.tensor_scalar_mul(o, p[:, :], sc),
            waits=[(s_pe4, t_mm_tok[blk]), (s_ac4, r9)], inc=s_y4, predelay=400)

    wdma4, wcast4 = {}, {}
    ffn_mm_tok = {}
    lora_tok = {}
    base_ev_tok = {}
    comb_tok = {}
    row_free = s_y4.n
    ln2_done_tok = 0
    wcast4[(-2, 0)] = wcast4[(-2, 1)] = abprep
    wcast4[(-1, 0)] = wcast4[(-1, 1)] = abprep
    fetch_w(wh, 0, 0, wdma4, wcast4, ffn_mm_tok)
    fetch_w(wh, 1, 1, wdma4, wcast4, ffn_mm_tok)
    for blk in range(2):
        for oc in range(DC):
            gi = blk * DC + oc
            wmm = [(s_wc3, wcast4[gi])]
            if gi >= 2:
                wmm.append((s_y4, comb_tok[gi - 2]))
            tok = None
            for dc in range(DC):
                tok = ctx.op(
                    "tensor",
                    lambda e, o=pb[gi % 2][:, :],
                           l=w_bf[gi % 3][:, dc * 128:(dc + 1) * 128],
                           r=x1[:, dc * 1024 + blk * 512: dc * 1024 + (blk + 1) * 512],
                           st=(dc == 0), sp=(dc == DC - 1): e.matmul(o, l, r, start=st, stop=sp),
                    waits=(wmm if dc == 0 else ()),
                    inc=(s_pe4 if dc == DC - 1 else None))
            ffn_mm_tok[gi] = tok
            wl = [(s_y4, ts_tok[blk])]
            if gi >= 1:
                wl.append((s_y4, comb_tok.get(gi - 1, 0)))
            lora_tok[gi] = ctx.op(
                "tensor",
                lambda e, l=blt_bf[:, oc * 128:(oc + 1) * 128],
                       r=t_bf[:, blk * 512:(blk + 1) * 512]:
                    e.matmul(pb[4][:, :], l, r, start=True, stop=True),
                waits=wl, inc=s_pe4)
            wbe = [(s_pe4, ffn_mm_tok[gi])]
            if gi >= 2:
                wbe.append((s_y4, comb_tok[gi - 2]))
            base_ev_tok[gi] = ctx.op(
                "scalar",
                lambda e, o=resid[:, (gi % 2) * 512:(gi % 2 + 1) * 512],
                       p=pb[gi % 2][:, :], bi=btile_sb[:, oc:oc + 1]:
                    e.activation(o, p, AF.Identity, bias=bi),
                waits=wbe, inc=s_ac4)
            wcb = [(s_pe4, lora_tok[gi]), (s_ac3, base_ev_tok[gi])]
            if oc == 0 and blk == 1:
                wcb.append((s_y4, ln2_done_tok))
            ctx.op("vector",
                   lambda e, p=pb[4][:, :], bs=resid[:, (gi % 2) * 512:(gi % 2 + 1) * 512]:
                       e.tensor_add(ytmpf[:, :], p, bs),
                   waits=wcb, predelay=150)
            comb_tok[gi] = yready[(blk, oc)] = ctx.op(
                "vector",
                lambda e, o=attn_all[:, oc * 512:(oc + 1) * 512],
                       x=x1[:, oc * 1024 + blk * 512: oc * 1024 + (blk + 1) * 512]:
                    e.tensor_add(o, x, ytmpf[:, :]),
                inc=s_y4)
            if gi + 2 < 2 * DC:
                fetch_w(wh, (gi + 2) % DC, gi + 2, wdma4, wcast4, ffn_mm_tok)
        _, _, _ = emit_ln(blk, row_free, out_final=True,
                          ysl_fn=lambda oc: attn_all[:, oc * 512:(oc + 1) * 512])
        row_free = s_y4.n
        ln2_done_tok = s_y4.n

    ctx.wait("gpsimd", (s_od[0], s_od[0].n), (s_od[1], s_od[1].n))

    with allow_ici_all_to_all():
        with nc.Block() as block:
            block.sync(lambda e: [f(e) for f in ctx.cmds["sync"]])
            block.tensor(lambda e: [f(e) for f in ctx.cmds["tensor"]])
            block.vector(lambda e: [f(e) for f in ctx.cmds["vector"]])
            block.scalar(lambda e: [f(e) for f in ctx.cmds["scalar"]])
            block.gpsimd(lambda e: [f(e) for f in ctx.cmds["gpsimd"]])
    return nc


def _host_prep(L0_attn_x, L0_attn_Wq, L0_attn_Wk, L0_attn_Wv, L0_attn_Wo,
               L0_ffn_W, L0_ffn_b, L0_ffn_A, L0_ffn_B, L0_ffn_router_W,
               L0_ffn_router_b):
    f32 = np.float32
    x = np.ascontiguousarray(L0_attn_x, dtype=f32)
    xT = np.ascontiguousarray(x.reshape(NT, D).T)
    woh = np.ascontiguousarray(
        np.asarray(L0_attn_Wo, f32).reshape(DC, 128, DC, 128).transpose(0, 3, 2, 1)
    ).reshape(DC, 128, D)
    wh = np.ascontiguousarray(
        np.asarray(L0_ffn_W, f32).reshape(DC, 128, DC, 128).transpose(0, 3, 2, 1)
    ).reshape(DC, 128, D)
    A_all = np.asarray(L0_ffn_A, f32).reshape(128, D)
    ah = np.ascontiguousarray(A_all.reshape(128, DC, 128).transpose(1, 2, 0))
    blt = np.ascontiguousarray(np.asarray(L0_ffn_B, f32).transpose(0, 2, 1).reshape(128, D))
    rwt = np.ascontiguousarray(
        np.asarray(L0_ffn_router_W, f32).T.reshape(DC, 128, 8).transpose(1, 0, 2)
    ).reshape(128, DC * 8)
    rbias = np.ascontiguousarray(np.asarray(L0_ffn_router_b, f32).reshape(1, 8))
    btile = np.ascontiguousarray(np.asarray(L0_ffn_b, f32).reshape(DC, 128).T)
    cosT, sinT = _rope_tables()

    wq = np.asarray(L0_attn_Wq, f32)
    wk = np.asarray(L0_attn_Wk, f32)
    wv = np.asarray(L0_attn_Wv, f32)

    in_maps = []
    for c in range(NCORES):
        wrows = np.concatenate([wq[512 * c:512 * (c + 1)],
                                wk[128 * c:128 * (c + 1)],
                                wv[128 * c:128 * (c + 1)]], axis=0)
        wqkv_h = np.ascontiguousarray(wrows.T).reshape(DC, 128, 768)
        xownT = np.ascontiguousarray(
            np.concatenate([xT[:, 512 * c:512 * c + 512],
                            xT[:, T + 512 * c: T + 512 * c + 512]], axis=1))
        rmask = np.zeros((1, 2 * OWN), f32)
        for blk in range(2):
            for u in range(OWN):
                tpos = 512 * c + u
                if Q_START <= tpos < Q_END:
                    rmask[0, blk * OWN + u] = 1.0 / QMEAN_N
        in_maps.append(dict(
            xT=xT, xownT=xownT, wqkv_h=wqkv_h, woh=woh, wh=wh, ah=ah, blt=blt,
            rwt=rwt, rbias=rbias, btile=btile, rmask=rmask, cosT=cosT, sinT=sinT,
        ))
    return in_maps


_CACHE = {}


def kernel(**inputs):
    in_maps = _host_prep(**inputs)
    if "nc" not in _CACHE:
        _CACHE["nc"] = build_nc(debug=False)
    nc = _CACHE["nc"]
    with allow_ici_all_to_all():
        res = run_bass_kernel_spmd(nc, in_maps, core_ids=list(range(NCORES)))
    out = np.empty((B, T, D), np.float32)
    for c in range(NCORES):
        ot = res.results[c]["out_t"]
        out[0, 512 * c:512 * (c + 1), :] = ot[:, 0:512].T
        out[1, 512 * c:512 * (c + 1), :] = ot[:, 512:1024].T
    return out


# version 12
# speedup vs baseline: 1.0167x; 1.0167x over previous
import os
os.environ.setdefault("TRN_ALLOW_ICI_ALL_TO_ALL", "1")
import math
import numpy as np
import ml_dtypes
import concourse.bass as bass
import concourse.mybir as mybir
from concourse.bass_utils import run_bass_kernel_spmd
from concourse.aot_env import allow_ici_all_to_all

F32 = mybir.dt.float32
BF16 = mybir.dt.bfloat16
AX = mybir.AxisListType.X
ALU = mybir.AluOpType
AF = mybir.ActivationFunctionType

NCORES = 8
B = 2
T = 4096
D = 4096
HD = 128
QH = 4
DC = D // 128
NT = B * T
TB = 256
NTB = NT // TB
TPB = NTB // B
OWN = 512
SM_SCALE = 1.0 / math.sqrt(HD)
SM_SHIFT = -30.0
NEG = -1.0e9
EPS = 1e-5
LORA_SCALE = 2.0
QMEAN_N = 3484.0
Q_START = 611
Q_END = 4095

ENGINES = ("sync", "tensor", "vector", "scalar", "gpsimd")


def _llama3_inv_freq(head_dim=128):
    base = 500000.0
    inv_freq = 1.0 / base ** (np.arange(0, head_dim, 2, dtype=np.float32) / head_dim)
    factor, low_f, high_f, old_ctx = 32.0, 1.0, 4.0, 8192
    low_wl, high_wl = old_ctx / low_f, old_ctx / high_f
    wavelen = 2 * math.pi / inv_freq
    inv_llama = np.where(wavelen > low_wl, inv_freq / factor, inv_freq)
    smooth = (old_ctx / wavelen - low_f) / (high_f - low_f)
    smoothed = (1 - smooth) * inv_llama / factor + smooth * inv_llama
    is_med = ~(wavelen < high_wl) & ~(wavelen > low_wl)
    return np.where(is_med, smoothed, inv_llama).astype(np.float32)


def _rope_tables():
    inv_freq = _llama3_inv_freq(HD)
    pos = np.arange(T, dtype=np.float32)
    freqs = pos[:, None] * inv_freq[None, :]
    emb = np.concatenate([freqs, freqs], axis=-1)
    cos = np.cos(emb).astype(np.float32)
    sin = np.sin(emb).astype(np.float32)
    return np.ascontiguousarray(cos.T), np.ascontiguousarray(sin.T)


class Sem:
    __slots__ = ("h", "name", "n")

    def __init__(self, h, name):
        self.h, self.name, self.n = h, name, 0


class Ctx:
    def __init__(self, nc):
        self.nc = nc
        self.cmds = {e: [] for e in ENGINES}
        self.last_wait = {e: {} for e in ENGINES}
        self._barn = 0

    def sem(self, name):
        return Sem(self.nc.alloc_semaphore(name), name)

    def _dedup(self, engine, waits):
        lw = self.last_wait[engine]
        ws = []
        for s, v in waits:
            if v is None or v <= 0:
                continue
            if lw.get(s.name, -1) < v:
                ws.append((s.h, v))
                lw[s.name] = v
        return ws

    def op(self, engine, fn, waits=(), inc=None, inck=1, predelay=0):
        ws = self._dedup(engine, waits)
        tok = None
        ih = None
        if inc is not None:
            inc.n += inck
            tok = inc.n
            ih = inc.h
        pd = predelay if ws else 0

        def emitter(eng, ws=ws, fn=fn, ih=ih, inck=inck, pd=pd):
            for h, v in ws:
                eng.wait_ge(h, v)
            if pd:
                eng.nop(cycle_cnt=pd)
            instr = fn(eng)
            if ih is not None:
                instr.then_inc(ih, inck)

        self.cmds[engine].append(emitter)
        return tok

    def wait(self, engine, *sem_vals):
        ws = self._dedup(engine, sem_vals)
        if not ws:
            return

        def emitter(eng, ws=ws):
            for h, v in ws:
                eng.wait_ge(h, v)

        self.cmds[engine].append(emitter)

    def barrier(self):
        self._barn += 1
        bar = self.sem(f"bar{self._barn}")
        for e in ENGINES:
            self.op(e, lambda eng: eng.nop(), inc=bar)
        tgt = bar.n
        for e in ENGINES:
            self.wait(e, (bar, tgt))


def build_nc(debug=False, phases=4):
    nc = bass.Bass("TRN2", target_bir_lowering=False)
    ctx = Ctx(nc)

    xT = nc.dram_tensor("xT", [D, NT], F32, kind="ExternalInput").ap()
    xownT = nc.dram_tensor("xownT", [D, 2 * OWN], F32, kind="ExternalInput").ap()
    wqkv_h = nc.dram_tensor("wqkv_h", [DC, 128, 6 * 128], F32, kind="ExternalInput").ap()
    woh = nc.dram_tensor("woh", [DC, 128, D], F32, kind="ExternalInput").ap()
    wh = nc.dram_tensor("wh", [DC, 128, D], F32, kind="ExternalInput").ap()
    ah = nc.dram_tensor("ah", [DC, 128, 128], F32, kind="ExternalInput").ap()
    blt_in = nc.dram_tensor("blt", [128, D], F32, kind="ExternalInput").ap()
    rwt_in = nc.dram_tensor("rwt", [128, DC * 8], F32, kind="ExternalInput").ap()
    rbias_in = nc.dram_tensor("rbias", [1, 8], F32, kind="ExternalInput").ap()
    btile_in = nc.dram_tensor("btile", [128, DC], F32, kind="ExternalInput").ap()
    rmask_in = nc.dram_tensor("rmask", [1, 2 * OWN], F32, kind="ExternalInput").ap()
    cosT = nc.dram_tensor("cosT", [128, T], F32, kind="ExternalInput").ap()
    sinT = nc.dram_tensor("sinT", [128, T], F32, kind="ExternalInput").ap()

    out_t = nc.dram_tensor("out_t", [D, 2 * OWN], F32, kind="ExternalOutput").ap()
    dbg = {}
    if debug:
        dbg["x1"] = nc.dram_tensor("dbg_x1", [128, DC * 2 * OWN], BF16, kind="ExternalOutput").ap()
        dbg["q"] = nc.dram_tensor("dbg_q", [QH * 128, NT], BF16, kind="ExternalOutput").ap()
        dbg["k"] = nc.dram_tensor("dbg_k", [128, NT], BF16, kind="ExternalOutput").ap()
        dbg["v"] = nc.dram_tensor("dbg_v", [128, NT], BF16, kind="ExternalOutput").ap()
        dbg["attn"] = nc.dram_tensor("dbg_attn", [2 * NCORES * 512, 512], BF16, kind="ExternalOutput").ap()
        dbg["routing"] = nc.dram_tensor("dbg_routing", [128, 2], F32, kind="ExternalOutput").ap()
        dbg["rsums"] = nc.dram_tensor("dbg_rsums", [128, 2 * DC], F32, kind="ExternalOutput").ap()
        dbg["rsumsg"] = nc.dram_tensor("dbg_rsumsg", [128, 2 * DC], F32, kind="ExternalOutput").ap()
        dbg["rsm"] = nc.dram_tensor("dbg_rsm", [8, 32], F32, kind="ExternalOutput").ap()

    q_dram = nc.dram_tensor("q_dram", [QH, 128, NT], BF16).ap()
    a2a_in = [nc.dram_tensor(f"a2a_in{b}", [NCORES * 512, 512], BF16).ap() for b in range(B)]
    a2a_out = [nc.dram_tensor(f"a2a_out{b}", [NCORES * 512, 512], BF16).ap() for b in range(B)]
    rt_in = nc.dram_tensor("rt_in", [128, 2 * DC], F32).ap()
    rt_out = nc.dram_tensor("rt_out", [128, 2 * DC], F32).ap()

    tri = np.where(np.arange(128)[None, :] >= np.arange(128)[:, None], 0.0, NEG)
    causal_c = nc.inline_tensor(tri.astype(ml_dtypes.bfloat16), "causal_c").ap()
    ident_bf_c = nc.inline_tensor(np.eye(128, dtype=ml_dtypes.bfloat16), "ident_bf_c").ap()
    ident_f_c = nc.inline_tensor(np.eye(128, dtype=np.float32), "ident_f_c").ap()
    ones_col_c = nc.inline_tensor(np.ones((128, 1), dtype=ml_dtypes.bfloat16), "ones_col_c").ap()
    ones_row_c = nc.inline_tensor(np.ones((1, 128), dtype=np.float32), "ones_row_c").ap()
    e2np = np.zeros((8, 128), dtype=np.float32)
    for e in range(8):
        e2np[e, e * 16:(e + 1) * 16] = LORA_SCALE
    e2_c = nc.inline_tensor(e2np, "e2_c").ap()

    AS = nc.alloc_sbuf_tensor
    causal = AS("causal", [128, 128], BF16).ap()
    identb = AS("identb", [128, 128], BF16).ap()
    identf = AS("identf", [128, 128], F32).ap()
    ones_col = AS("ones_col", [128, 1], BF16).ap()
    ones_row = AS("ones_row", [1, 128], F32).ap()
    e2_sb = AS("e2_sb", [8, 128], F32).ap()
    btile_sb = AS("btile_sb", [128, DC], F32).ap()
    rwt_sb = AS("rwt_sb", [128, DC * 8], F32).ap()
    rbias_sb = AS("rbias_sb", [1, 8], F32).ap()
    rmask_sb = AS("rmask_sb", [1, 2 * OWN], F32).ap()

    rsums = AS("rsums", [128, 2 * DC], F32).ap()
    rsums_g = AS("rsums_g", [128, 2 * DC], F32).ap()
    routing_sc = AS("routing_sc", [128, 2], F32).ap()
    rsm = AS("rsm", [8, 32], F32).ap()
    shift_sb = AS("shift_sb", [128, 1], F32).ap()
    eps_sb = AS("eps_sb", [128, 1], F32).ap()
    row_a = AS("row_a", [1, 512], F32).ap()
    row_b = AS("row_b", [1, 512], F32).ap()
    row_c = AS("row_c", [1, 512], F32).ap()
    stage = AS("stage", [128, 8192], F32).ap()

    AP_ = nc.alloc_psum_tensor
    pb = [AP_(f"pb{i}", [128, 512], F32).ap() for i in range(7)]
    pb7_h = AP_("pb7", [128, 512], F32)
    pb.append(pb7_h.ap())
    pvt = pb7_h.bitcast(BF16).ap()

    g_kv = []
    def _alloc(name, shape, dt, bag):
        g = nc.sbuf_tensor(name, shape, dt)
        ap = g.__enter__().ap()
        bag.append(g)
        return ap

    wqkv_bf = _alloc("wqkv_bf", [128, DC * 768], BF16, g_kv)
    cos_bf = _alloc("cos_bf", [128, T], BF16, g_kv)
    sin_bf = _alloc("sin_bf", [128, T], BF16, g_kv)
    Karena = _alloc("Karena", [128, NT], BF16, g_kv)
    Vtm = _alloc("Vtm", [128, NT], BF16, g_kv)
    g_p1 = []
    x_bf = [_alloc("x_bf0", [128, DC * TB], BF16, g_p1),
            _alloc("x_bf1", [128, DC * TB], BF16, g_p1),
            _alloc("x_bf2", [128, DC * TB], BF16, g_p1)]
    qraw = _alloc("qraw", [128, 2 * TB], BF16, g_p1)
    qrope = _alloc("qrope", [128, 2 * TB], BF16, g_p1)
    rtmp = _alloc("rtmp", [128, TB], BF16, g_p1)
    vraw = _alloc("vraw", [128, 2 * TB], BF16, g_p1)

    def pad_tok(engine, sem, cycles=300):
        return ctx.op(engine, lambda e, c=cycles: e.nop(cycle_cnt=c), inc=sem)

    def dma(engine, out, in_, sem, waits=()):
        return ctx.op(engine, lambda e, o=out, i=in_: e.dma_start(out=o, in_=i),
                      waits=waits, inc=sem, inck=16)

    s_ldc = ctx.sem("s_ldc")
    s_ld0 = ctx.sem("s_ld0")
    s_ld1 = ctx.sem("s_ld1")
    s_cast = ctx.sem("s_cast")

    for (dst, src) in [(causal, causal_c), (identb, ident_bf_c), (identf, ident_f_c),
                       (ones_col, ones_col_c), (ones_row, ones_row_c), (e2_sb, e2_c),
                       (btile_sb, btile_in), (rwt_sb, rwt_in), (rbias_sb, rbias_in),
                       (rmask_sb, rmask_in)]:
        dma("sync", dst[:, :], src[:, :], s_ldc)

    t_cos = dma("sync", stage[:, 0:T], cosT[:, :], s_ld0)
    t_cc = ctx.op("vector", lambda e: e.tensor_copy(cos_bf[:, :], stage[:, 0:T]),
                  waits=[(s_ld0, t_cos)], inc=s_cast)
    t_sin = dma("sync", stage[:, 0:T], sinT[:, :], s_ld0, waits=[(s_cast, t_cc)])
    t_sc = ctx.op("vector", lambda e: e.tensor_copy(sin_bf[:, :], stage[:, 0:T]),
                  waits=[(s_ld0, t_sin)], inc=s_cast)
    ctx.op("vector", lambda e: e.memset(rsums[:, :], 0.0), inc=s_cast)
    ctx.op("vector", lambda e: e.memset(shift_sb[:, :], SM_SHIFT), inc=s_cast)
    ctx.op("vector", lambda e: e.memset(eps_sb[:, :], EPS), inc=s_cast)
    cs_done = s_cast.n

    half_free = {0: cs_done, 1: cs_done}
    for i in range(16):
        hf = i % 2
        st_v = stage[:, hf * 2048: hf * 2048 + 1536].rearrange("p (c f) -> p c f", c=2)
        st_f = stage[:, hf * 2048: hf * 2048 + 1536]
        t_d = dma("sync", st_v, wqkv_h[2 * i:2 * i + 2, :, :].rearrange("c p f -> p c f"),
                  (s_ld0 if hf == 0 else s_ld1), waits=[(s_cast, half_free[hf])])
        half_free[hf] = ctx.op(
            "vector",
            lambda e, o=wqkv_bf[:, i * 1536:(i + 1) * 1536], s=st_f: e.tensor_copy(o, s),
            waits=[((s_ld0 if hf == 0 else s_ld1), t_d)], inc=s_cast)
    wq_cast = s_cast.n

    ctx.wait("tensor", (s_ldc, s_ldc.n))
    ctx.wait("scalar", (s_ldc, s_ldc.n))
    ctx.wait("vector", (s_ldc, s_ldc.n))
    ctx.barrier()

    s_xd = [ctx.sem(f"s_xd{i}") for i in range(4)]
    s_xc = ctx.sem("s_xc")
    s_pe1 = ctx.sem("s_pe1")
    s_ac1 = ctx.sem("s_ac1")
    s_rp1 = ctx.sem("s_rp1")
    s_qd = [ctx.sem("s_qd0"), ctx.sem("s_qd1")]

    xT_v = xT.rearrange("(c p) t -> p c t", p=128)

    xdma_tok, xcast_tok, xfree_tok = {}, {}, {}
    evac_tok, rope_tok, qdma_tok = {}, {}, {}
    vt_tok = {}
    evac_tok_psum = {}

    def prev_qraw(tb, of):
        chain = [0, 2, 4] if of % 2 == 0 else [1, 3]
        i = chain.index(of)
        if i > 0:
            return (tb, chain[i - 1])
        return (tb - 1, chain[-1]) if tb > 0 else None

    def prev_qrope(tb, of):
        chain = [0, 2] if of % 2 == 0 else [1, 3]
        i = chain.index(of)
        if i > 0:
            return (tb, chain[i - 1])
        return (tb - 1, chain[-1]) if tb > 0 else None

    def emit_vtrans(tprev):
        vsl = vraw[:, (tprev % 2) * TB:(tprev % 2 + 1) * TB]
        pslot = (tprev % 2) * 512
        wv = [(s_ac1, evac_tok[(tprev, 5)])]
        if tprev >= 2:
            wv.append((s_ac1, vt_tok[("ac", tprev - 2)]))
        tok = None
        for i in range(TB // 128):
            tok = ctx.op(
                "tensor",
                lambda e, o=pvt[:, pslot + i * 128: pslot + (i + 1) * 128],
                       s=vsl[:, i * 128:(i + 1) * 128]: e.transpose(o, s, identb[:, :]),
                waits=(wv if i == 0 else ()), inc=s_pe1, predelay=(200 if i == 0 else 0))
        vt_tok[("pe", tprev)] = tok
        vt_tok[("ac", tprev)] = ctx.op(
            "scalar",
            lambda e, o=Vtm[:, tprev * TB:(tprev + 1) * TB],
                   s=pvt[:, pslot:pslot + TB]: e.copy(o, s),
            waits=[(s_pe1, tok)], inc=s_ac1)

    for tb in range(NTB):
        par = tb % 3
        for hf in range(4):
            st_v = stage[:, hf * 2048:(hf + 1) * 2048]
            waits = []
            if tb == 0:
                waits.append((s_cast, wq_cast))
            if tb >= 1:
                waits.append((s_xc, xcast_tok[(tb - 1, hf)]))
            if tb >= 3:
                waits.append((s_pe1, xfree_tok[tb - 3]))
            xdma_tok[(tb, hf)] = dma(
                "sync",
                st_v.rearrange("p (c t) -> p c t", c=8),
                xT_v[:, 8 * hf:8 * hf + 8, tb * TB:(tb + 1) * TB],
                s_xd[hf], waits=waits)
            xcast_tok[(tb, hf)] = ctx.op(
                "vector",
                lambda e, o=x_bf[par][:, hf * 8 * TB:(hf + 1) * 8 * TB], s=st_v:
                    e.tensor_copy(o, s),
                waits=[(s_xd[hf], xdma_tok[(tb, hf)])], inc=s_xc)

        for of in range(6):
            gidx = tb * 6 + of
            slot = gidx % 3
            ps = pb[slot]
            w0 = [(s_xc, xcast_tok[(tb, 3)]), (s_cast, wq_cast)]
            if gidx >= 3:
                w0.append((s_ac1, evac_tok_psum[gidx - 3]))
            mm_tok = None
            for dc in range(DC):
                mm_tok = ctx.op(
                    "tensor",
                    lambda e, o=ps[:, 0:TB],
                           l=wqkv_bf[:, dc * 768 + of * 128: dc * 768 + (of + 1) * 128],
                           r=x_bf[par][:, dc * TB:(dc + 1) * TB],
                           st=(dc == 0), sp=(dc == DC - 1): e.matmul(o, l, r, start=st, stop=sp),
                    waits=(w0 if dc == 0 else ()),
                    inc=(s_pe1 if dc == DC - 1 else None))
            if of == 5:
                xfree_tok[tb] = mm_tok

            if of <= 4:
                qpar = of % 2
                if of < 4:
                    rawsl = qraw[:, qpar * TB:(qpar + 1) * TB]
                    ropesl = qrope[:, qpar * TB:(qpar + 1) * TB]
                else:
                    rawsl = qraw[:, 0:TB]
                    ropesl = Karena[:, tb * TB:(tb + 1) * TB]
                w2 = [(s_pe1, mm_tok)]
                pq = prev_qraw(tb, of)
                if pq is not None and pq in rope_tok:
                    w2.append((s_rp1, rope_tok[pq]))
                evac_tok_psum[gidx] = evac_tok[(tb, of)] = ctx.op(
                    "scalar", lambda e, o=rawsl, s=ps[:, 0:TB]: e.copy(o, s),
                    waits=w2, inc=s_ac1, predelay=150)
                pos0 = (tb % TPB) * TB
                cosv = cos_bf[:, pos0:pos0 + TB]
                sinv = sin_bf[:, pos0:pos0 + TB]
                wr = [(s_ac1, evac_tok[(tb, of)])]
                if of < 4:
                    pr = prev_qrope(tb, of)
                    if pr is not None and pr in qdma_tok:
                        wr.append((s_qd[of % 2], qdma_tok[pr]))
                ctx.op("vector",
                       lambda e, o=rtmp[0:64, :], i=rawsl[64:128, :]:
                           e.tensor_scalar_mul(o, i, -1.0), waits=wr, predelay=200)
                ctx.op("vector",
                       lambda e, o=rtmp[64:128, :], i=rawsl[0:64, :]: e.tensor_copy(o, i))
                ctx.op("vector",
                       lambda e, o=ropesl, i=rawsl, c=cosv: e.tensor_mul(o, i, c))
                ctx.op("vector",
                       lambda e, s=sinv: e.tensor_mul(rtmp[:, :], rtmp[:, :], s))
                rope_tok[(tb, of)] = ctx.op(
                    "vector", lambda e, o=ropesl: e.tensor_add(o, o, rtmp[:, :]),
                    inc=s_rp1)
                if of < 4:
                    qdma_tok[(tb, of)] = dma(
                        "gpsimd", q_dram[of, :, tb * TB:(tb + 1) * TB], ropesl,
                        s_qd[of % 2], waits=[(s_rp1, rope_tok[(tb, of)])])
            else:
                vsl = vraw[:, (tb % 2) * TB:(tb % 2 + 1) * TB]
                w2 = [(s_pe1, mm_tok)]
                if tb >= 2:
                    w2.append((s_pe1, vt_tok[("pe", tb - 2)]))
                evac_tok_psum[gidx] = evac_tok[(tb, 5)] = ctx.op(
                    "scalar", lambda e, o=vsl, s=ps[:, 0:TB]: e.copy(o, s),
                    waits=w2, inc=s_ac1, predelay=150)
        if tb >= 1:
            emit_vtrans(tb - 1)
    emit_vtrans(NTB - 1)

    ctx.barrier()

    def emit_dumps_and_finish(s_fin):
        dma("gpsimd", dbg["k"][:, :], Karena[:, :], s_fin)
        dma("gpsimd", dbg["v"][:, :], Vtm[:, :], s_fin)
        dma("gpsimd", dbg["q"].rearrange("(h p) t -> h p t", p=128), q_dram[:, :, :], s_fin)
        ctx.wait("gpsimd", (s_fin, s_fin.n))
        with allow_ici_all_to_all():
            with nc.Block() as block:
                block.sync(lambda e: [f(e) for f in ctx.cmds["sync"]])
                block.tensor(lambda e: [f(e) for f in ctx.cmds["tensor"]])
                block.vector(lambda e: [f(e) for f in ctx.cmds["vector"]])
                block.scalar(lambda e: [f(e) for f in ctx.cmds["scalar"]])
                block.gpsimd(lambda e: [f(e) for f in ctx.cmds["gpsimd"]])
        return nc

    if phases == 1:
        return emit_dumps_and_finish(ctx.sem("s_fin"))

    for g in reversed(g_p1):
        g.__exit__(None, None, None)
    g_p1 = []
    g_p2 = []
    qu = [_alloc("qu0", [128, T], BF16, g_p2), _alloc("qu1", [128, T], BF16, g_p2)]
    pt = _alloc("pt", [128, 5 * 512], BF16, g_p2)
    a2at = _alloc("a2at", [128, 2 * 512], BF16, g_p2)
    rbc = _alloc("rbc", [128, 512], F32, g_p2)
    recipr = _alloc("recipr", [1, 512], F32, g_p2)

    s_qld = [ctx.sem("s_qld0"), ctx.sem("s_qld1")]
    s_st = ctx.sem("s_st")
    s_msk = ctx.sem("s_msk")
    s_exp = ctx.sem("s_exp")
    s_av = ctx.sem("s_av")
    s_nrm = ctx.sem("s_nrm")
    s_bcst = ctx.sem("s_bcst")
    s_bcc = ctx.sem("s_bcc")
    s_a2d = [ctx.sem("s_a2d0"), ctx.sem("s_a2d1")]
    s_rtd = ctx.sem("s_rtd")
    s_cc = ctx.sem("s_cc")

    st_tok, exp_tok, av_tok = {}, {}, {}
    unit_last_st = {}
    slot_bc_guard = {}
    avb_free = {}
    sum_free = 0
    kchunk = 0
    nblk = 0

    for u in range(2 * QH):
        b, h = u // QH, u % QH
        par = u % 2
        wq_ = [(s_qd[0], s_qd[0].n), (s_qd[1], s_qd[1].n)]
        if u < 2:
            wq_ += [(s_pe1, s_pe1.n), (s_rp1, s_rp1.n)]
        if u >= 2:
            wq_.append((s_st, unit_last_st[u - 2]))
        t_q = dma("sync", qu[par][:, :], q_dram[h, :, b * T:(b + 1) * T],
                  s_qld[par], waits=wq_)
        descs = []
        for qb in range(8):
            nch = 4 * qb + 4
            for c in range(nch):
                sufs = 0 if c < 4 * qb else (c - 4 * qb) * 128
                descs.append((qb, c, sufs, c >= 4 * qb, c == nch - 1))
        N = len(descs)
        LA = 4
        NS = 5
        SLOTS = (0, 1, 2, 3, 7)
        k0 = kchunk

        def emit_st(i):
            qb, c, sufs, diag, last = descs[i]
            kk = k0 + i
            slot = SLOTS[kk % NS]
            wst = [(s_qld[par], t_q)]
            if kk >= NS:
                wst.append((s_exp, exp_tok[kk - NS]))
            if slot_bc_guard.get(slot):
                wst.append((s_bcc, slot_bc_guard.pop(slot)))
            ctx.op(
                "tensor",
                lambda e, o=pb[slot][:, sufs:512],
                       l=Karena[:, b * T + c * 128: b * T + (c + 1) * 128],
                       r=qu[par][:, qb * 512 + sufs:(qb + 1) * 512],
                       sp=(not diag): e.matmul(o, l, r, start=True, stop=sp),
                waits=wst, inc=(None if diag else s_st))
            if diag:
                ctx.op(
                    "tensor",
                    lambda e, o=pb[slot][:, sufs:sufs + 128]:
                        e.matmul(o, identb[:, :], causal[:, :], start=False, stop=True),
                    inc=s_st)
            st_tok[kk] = s_st.n
            unit_last_st[u] = s_st.n

        for j in range(min(LA, N)):
            emit_st(j)
        for i in range(N):
            qb, c, sufs, diag, last = descs[i]
            kk = k0 + i
            slot = SLOTS[kk % NS]
            nch = 4 * qb + 4
            avb = pb[4 + nblk % 2]
            we = [(s_st, st_tok[kk])]
            if kk >= NS:
                we.append((s_av, av_tok[kk - NS]))
            exp_tok[kk] = ctx.op(
                "scalar",
                lambda e, o=pt[:, (kk % NS) * 512 + sufs:(kk % NS + 1) * 512],
                       s=pb[slot][:, sufs:512]:
                    e.activation(o, s, AF.Exp, bias=shift_sb[:, 0:1], scale=SM_SCALE),
                waits=we, inc=s_exp, predelay=(150 if diag else 0))
            wav = [(s_exp, exp_tok[kk])]
            if c == 0:
                wav.append((s_nrm, avb_free.get(nblk % 2, 0)))
            ctx.op(
                "tensor",
                lambda e, o=avb[:, sufs:512],
                       l=Vtm[:, (b * 32 + c) * 128:(b * 32 + c + 1) * 128],
                       r=pt[:, (kk % NS) * 512 + sufs:(kk % NS + 1) * 512],
                       st=(c == 0), sp=(c == nch - 1): e.matmul(o, l, r, start=st, stop=sp),
                waits=wav)
            wsum = []
            if c == 0:
                wsum.append((s_nrm, sum_free))
            av_tok[kk] = ctx.op(
                "tensor",
                lambda e, o=pb[6][0:1, sufs:512],
                       r=pt[:, (kk % NS) * 512 + sufs:(kk % NS + 1) * 512],
                       st=(c == 0), sp=(c == nch - 1):
                    e.matmul(o, ones_col[:, :], r, start=st, stop=sp),
                waits=wsum, inc=s_av)
            if i + LA < N:
                emit_st(i + LA)
            if last:
                rc_tok = ctx.op(
                    "vector", lambda e: e.reciprocal(recipr[:, :], pb[6][0:1, :]),
                    waits=[(s_av, av_tok[kk]), (s_bcst, s_bcst.n)], inc=s_nrm,
                    predelay=150)
                sum_free = rc_tok
                bslot = slot
                bc_tok = ctx.op(
                    "tensor",
                    lambda e, o=pb[bslot][:, :]: e.matmul(o, ones_row[0:1, :], recipr[0:1, :],
                                                          start=True, stop=True),
                    waits=[(s_nrm, rc_tok), (s_exp, exp_tok[kk]), (s_bcc, s_bcc.n)],
                    inc=s_bcst, predelay=300)
                slot_bc_guard[bslot] = None
                wbcc = [(s_bcst, bc_tok)]
                if nblk >= 1:
                    wbcc.append((s_nrm, s_nrm.n))
                bcc_tok = ctx.op(
                    "scalar", lambda e, o=pb[bslot]: e.copy(rbc[:, :], o[:, :]),
                    waits=wbcc, inc=s_bcc)
                slot_bc_guard[bslot] = bcc_tok
                asl = a2at[:, (nblk % 2) * 512:(nblk % 2 + 1) * 512]
                wn = [(s_bcc, bcc_tok)]
                if nblk >= 2:
                    wn.append((s_a2d[nblk % 2], 16 * ((nblk - 2) // 2 + 1)))
                n_tok = ctx.op(
                    "vector", lambda e, o=asl, a=avb: e.tensor_mul(o, a[:, :], rbc[:, :]),
                    waits=wn, inc=s_nrm, predelay=300)
                avb_free[nblk % 2] = n_tok
                dma("gpsimd", a2a_in[b][qb * 512 + h * 128: qb * 512 + (h + 1) * 128, :], asl,
                    s_a2d[nblk % 2], waits=[(s_nrm, n_tok)])
                nblk += 1
        kchunk += N
        if h == QH - 1:
            ctx.wait("gpsimd", (s_a2d[0], s_a2d[0].n), (s_a2d[1], s_a2d[1].n))
            ctx.op("gpsimd",
                   lambda e, i=a2a_in[b], o=a2a_out[b]: e.collective_compute(
                       "AllToAll", ALU.bypass,
                       replica_groups=[list(range(NCORES))],
                       ins=[i.opt()], outs=[o.opt()]),
                   inc=s_cc)

    ctx.barrier()
    if debug:
        s_dbg = ctx.sem("s_dbg")
        dma("gpsimd", dbg["k"][:, :], Karena[:, :], s_dbg)
        dma("gpsimd", dbg["v"][:, :], Vtm[:, :], s_dbg)
        dma("gpsimd", dbg["q"].rearrange("(h p) t -> h p t", p=128), q_dram[:, :, :], s_dbg)
        for b_ in range(B):
            dma("gpsimd", dbg["attn"][b_ * NCORES * 512:(b_ + 1) * NCORES * 512, :],
                a2a_out[b_][:, :], s_dbg, waits=[(s_cc, b_ + 1)])
        ctx.wait("gpsimd", (s_dbg, s_dbg.n))
        ctx.barrier()
    for g in reversed(g_p2):
        g.__exit__(None, None, None)
    for g in reversed(g_kv):
        g.__exit__(None, None, None)

    g_p3 = []
    x1 = _alloc("x1", [128, DC * 2 * OWN], BF16, g_p3)
    attn_all = _alloc("attn_all", [128, DC * 512], BF16, g_p3)
    w_bf = [_alloc("w_bf0", [128, 4096], BF16, g_p3), _alloc("w_bf1", [128, 4096], BF16, g_p3),
            _alloc("w_bf2", [128, 4096], BF16, g_p3)]
    resid = _alloc("resid", [128, 2 * 512], F32, g_p3)
    sqtile = _alloc("sqtile", [128, 2 * 512], BF16, g_p3)
    ytmpf = _alloc("ytmpf", [128, 512], F32, g_p3)
    bcab = _alloc("bcab", [128, 2 * 512], F32, g_p3)
    mask_bc = _alloc("mask_bc", [128, 512], F32, g_p3)
    t_bf = _alloc("t_bf", [128, 2 * 512], BF16, g_p3)
    at_bf = _alloc("at_bf", [128, DC * 128], BF16, g_p3)
    blt_bf = _alloc("blt_bf", [128, D], BF16, g_p3)
    osl = _alloc("osl", [128, 2 * 512], F32, g_p3)

    s_a2r = ctx.sem("s_a2r")
    s_wd3 = [ctx.sem(f"s_wd3{i}") for i in range(4)]
    s_wc3 = ctx.sem("s_wc3")
    s_pe3 = ctx.sem("s_pe3")
    s_y3 = ctx.sem("s_y3")
    s_ac3 = ctx.sem("s_ac3")
    s_rd3 = [ctx.sem("s_rd30"), ctx.sem("s_rd31")]
    s_od = [ctx.sem("s_od0"), ctx.sem("s_od1")]

    def fetch_w(src3d, oc, gi, wdma_tok, wcast_tok, mm_done_tok):
        wpar = gi % 3
        for hf in range(2):
            sidx = (gi * 2 + hf) % 4
            st_f = stage[:, sidx * 2048:(sidx + 1) * 2048]
            wd = []
            if (gi - 2, hf) in wcast_tok:
                wd.append((s_wc3, wcast_tok[(gi - 2, hf)]))
            wdma_tok[(gi, hf)] = dma("sync", st_f, src3d[oc, :, hf * 2048:(hf + 1) * 2048],
                                     s_wd3[sidx], waits=wd)
            wc = [(s_wd3[sidx], wdma_tok[(gi, hf)])]
            if hf == 0 and (gi - 3) in mm_done_tok:
                wc.append((s_pe3, mm_done_tok[gi - 3]))
            wcast_tok[(gi, hf)] = ctx.op(
                "vector",
                lambda e, o=w_bf[wpar][:, hf * 2048:(hf + 1) * 2048], s=st_f: e.tensor_copy(o, s),
                waits=wc, inc=s_wc3)
        wcast_tok[gi] = wcast_tok[(gi, 1)]

    def emit_ln(blk, bc_wait_tok, out_final, ysl_fn=None):
        sq_tok = {}
        stats_tok = {}
        if ysl_fn is None:
            ysl_fn = lambda oc: x1[:, oc * 1024 + blk * 512: oc * 1024 + (blk + 1) * 512]
        for oc in range(DC):
            ysl = ysl_fn(oc)
            wsq = [(s_y3, yready[(blk, oc)])]
            if oc >= 2:
                wsq.append((s_pe3, stats_tok[oc - 2]))
            sq_tok[oc] = ctx.op(
                "scalar",
                lambda e, o=sqtile[:, (oc % 2) * 512:(oc % 2 + 1) * 512], s=ysl:
                    e.activation(o, s, AF.Square),
                waits=wsq, inc=s_ac3, predelay=150)
            wst = [(s_y3, yready[(blk, oc)])]
            if oc == 0:
                wst.append((s_y3, bc_wait_tok))
            ctx.op("tensor",
                   lambda e, o=pb[2][0:1, :], r=ysl, st=(oc == 0), sp=(oc == DC - 1):
                       e.matmul(o, ones_col[:, :], r, start=st, stop=sp),
                   waits=wst)
            stats_tok[oc] = ctx.op(
                "tensor",
                lambda e, o=pb[3][0:1, :], r=sqtile[:, (oc % 2) * 512:(oc % 2 + 1) * 512],
                       st=(oc == 0), sp=(oc == DC - 1):
                    e.matmul(o, ones_col[:, :], r, start=st, stop=sp),
                waits=[(s_ac3, sq_tok[oc])], inc=s_pe3, predelay=150)
        t1 = ctx.op("vector", lambda e: e.tensor_scalar_mul(row_a[:, :], pb[2][0:1, :], 1.0 / D),
                    waits=[(s_pe3, stats_tok[DC - 1])], inc=s_y3, predelay=150)
        ctx.op("vector", lambda e: e.tensor_scalar_mul(row_b[:, :], pb[3][0:1, :], 1.0 / D))
        ctx.op("vector", lambda e: e.tensor_mul(row_c[:, :], row_a[:, :], row_a[:, :]))
        t2 = ctx.op("vector", lambda e: e.tensor_sub(row_b[:, :], row_b[:, :], row_c[:, :]),
                    inc=s_y3)
        ctx.op("scalar", lambda e: e.activation(row_c[:, :], row_b[:, :], AF.Ln, bias=eps_sb[0:1, 0:1]),
               waits=[(s_y3, t2)], inc=s_ac3, predelay=300)
        t4 = ctx.op("scalar", lambda e: e.activation(row_b[:, :], row_c[:, :], AF.Exp, scale=-0.5),
                    inc=s_ac3)
        t5 = ctx.op("vector",
                    lambda e: e.scalar_tensor_tensor(row_c[:, :], row_a[:, :], -1.0, row_b[:, :],
                                                     ALU.mult, ALU.mult),
                    waits=[(s_ac3, t4)], inc=s_y3, predelay=300)
        tba = ctx.op("tensor", lambda e: e.matmul(pb[4][:, :], ones_row[0:1, :], row_b[0:1, :],
                                                  start=True, stop=True),
                     waits=[(s_y3, t5), (s_ac3, s_ac3.n), (s_y3, s_y3.n)], inc=s_pe3,
                     predelay=300)
        tbc = ctx.op("tensor", lambda e: e.matmul(pb[5][:, :], ones_row[0:1, :], row_c[0:1, :],
                                                  start=True, stop=True), inc=s_pe3)
        tca = ctx.op("scalar", lambda e: e.copy(bcab[:, 0:512], pb[4][:, :]),
                     waits=[(s_pe3, tbc)], inc=s_ac3)
        tcc = ctx.op("scalar", lambda e: e.copy(bcab[:, 512:1024], pb[5][:, :]), inc=s_ac3)
        norm_toks = {}
        for oc in range(DC):
            ysl = ysl_fn(oc)
            if out_final:
                odst = osl[:, (oc % 2) * 512:(oc % 2 + 1) * 512]
                wn = [(s_ac3, tcc)]
                if oc >= 2:
                    wn.append((s_od[oc % 2], od_tok[(blk, oc - 2)]))
                elif blk >= 1:
                    wn.append((s_od[oc % 2], od_tok[(blk - 1, DC - 2 + oc)]))
                ctx.op("vector", lambda e, y=ysl, o=ytmpf: e.tensor_mul(o[:, :], y, bcab[:, 0:512]),
                       waits=wn, predelay=300)
                norm_toks[oc] = ctx.op(
                    "vector", lambda e, o=odst: e.tensor_add(o, ytmpf[:, :], bcab[:, 512:1024]),
                    inc=s_y3)
                od_tok[(blk, oc)] = dma(
                    "gpsimd", out_t[oc * 128:(oc + 1) * 128, blk * 512:(blk + 1) * 512],
                    odst, s_od[oc % 2], waits=[(s_y3, norm_toks[oc])])
            else:
                ctx.op("vector", lambda e, y=ysl: e.tensor_mul(ytmpf[:, :], y, bcab[:, 0:512]),
                       waits=[(s_ac3, tcc)], predelay=300)
                norm_toks[oc] = ctx.op(
                    "vector", lambda e, y=ysl: e.tensor_add(y, ytmpf[:, :], bcab[:, 512:1024]),
                    inc=s_y3)
        return tca, tcc, norm_toks

    yready = {}
    od_tok = {}
    wo_mm_tok = {}
    wo_y_tok = {}
    wdma3, wcast3 = {}, {}
    x1row_free = 0
    stt_done = 0
    for blk in range(2):
        ctx.wait("sync", (s_cc, blk + 1))
        for s in range(NCORES):
            wr = [(s_pe3, wo_mm_tok[DC - 1])] if (blk == 1 and s == 0) else []
            rb_tok = dma("sync",
                         attn_all[:, s * 2048:(s + 1) * 2048].rearrange(
                             "p (h t) -> p h t", h=4),
                         a2a_out[blk][s * 512:(s + 1) * 512, :].rearrange(
                             "(h p) t -> p h t", p=128),
                         s_a2r, waits=wr)
        if blk == 0:
            for hf_ in range(2):
                wcast3[(-2, hf_)] = None
                wcast3[(-1, hf_)] = None
            ctx.wait("sync", (s_xc, s_xc.n))
            fetch_w(woh, 0, 0, wdma3, wcast3, wo_mm_tok)
            fetch_w(woh, 1, 1, wdma3, wcast3, wo_mm_tok)
        for oc in range(DC):
            gi = blk * DC + oc
            rwt_ = []
            if gi >= 2:
                rwt_.append((s_y3, yready[(blk, oc - 2)] if oc >= 2 else yready[(blk - 1, DC - 2 + oc)]))
            rd_tok = dma("gpsimd", resid[:, (gi % 2) * 512:(gi % 2 + 1) * 512],
                         xownT[oc * 128:(oc + 1) * 128, blk * 512:(blk + 1) * 512],
                         s_rd3[gi % 2], waits=rwt_)
            wmm = [(s_a2r, rb_tok), (s_wc3, wcast3[gi])]
            if gi >= 2:
                wmm.append((s_y3, wo_y_tok[gi - 2]))
            tok = None
            for fc in range(DC):
                tok = ctx.op(
                    "tensor",
                    lambda e, o=pb[gi % 2][:, :],
                           l=w_bf[gi % 3][:, fc * 128:(fc + 1) * 128],
                           r=attn_all[:, fc * 512:(fc + 1) * 512],
                           st=(fc == 0), sp=(fc == DC - 1): e.matmul(o, l, r, start=st, stop=sp),
                    waits=(wmm if fc == 0 else ()),
                    inc=(s_pe3 if fc == DC - 1 else None))
            wo_mm_tok[gi] = tok
            wo_y_tok[gi] = yready[(blk, oc)] = ctx.op(
                "vector",
                lambda e, o=x1[:, oc * 1024 + blk * 512: oc * 1024 + (blk + 1) * 512],
                       p=pb[gi % 2][:, :], r=resid[:, (gi % 2) * 512:(gi % 2 + 1) * 512]:
                    e.tensor_add(o, p, r),
                waits=[(s_pe3, tok), (s_rd3[gi % 2], rd_tok)], inc=s_y3, predelay=150)
            if gi + 2 < 2 * DC:
                fetch_w(woh, (gi + 2) % DC, gi + 2, wdma3, wcast3, wo_mm_tok)
        tca, tcc, norm_toks = emit_ln(blk, x1row_free, out_final=False)
        x1row_free = s_y3.n
        tmb = ctx.op("tensor",
                     lambda e, r=rmask_sb[0:1, blk * 512:(blk + 1) * 512]:
                         e.matmul(pb[4][:, :], ones_row[0:1, :], r, start=True, stop=True),
                     waits=[(s_ac3, tca), (s_y3, stt_done)], inc=s_pe3)
        tmc = ctx.op("scalar",
                     lambda e: e.copy(mask_bc[:, :], pb[4][:, :]),
                     waits=[(s_pe3, tmb)], inc=s_ac3)
        for oc in range(DC):
            ysl = x1[:, oc * 1024 + blk * 512: oc * 1024 + (blk + 1) * 512]
            ctx.op("vector",
                   lambda e, y=ysl, o=rsums[:, blk * DC + oc: blk * DC + oc + 1],
                          m=mask_bc[:, :]:
                       e.scalar_tensor_tensor(ytmpf[:, :], y, 1.0, m, ALU.mult, ALU.mult,
                                              accum_out=o),
                   waits=[(s_y3, norm_toks[oc]), (s_ac3, tmc)],
                   predelay=300,
                   inc=(s_y3 if oc == DC - 1 else None))
        stt_done = s_y3.n
    rsums_done = s_y3.n

    t_ri = dma("gpsimd", rt_in[:, :], rsums[:, :], s_rtd, waits=[(s_y3, rsums_done)])
    ctx.wait("gpsimd", (s_rtd, t_ri))
    ctx.op("gpsimd",
           lambda e: e.collective_compute(
               "AllReduce", ALU.add, replica_groups=[list(range(NCORES))],
               ins=[rt_in.opt()], outs=[rt_out.opt()]),
           inc=s_cc)
    if debug:
        t_dx = dma("gpsimd", dbg["x1"][:, :], x1[:, :], s_od[0], waits=[(s_y3, rsums_done)])
        ctx.wait("gpsimd", (s_od[0], t_dx))

    ctx.barrier()

    s_pe4 = s_pe3
    s_y4 = s_y3
    s_ac4 = s_ac3

    t_rg = dma("gpsimd", rsums_g[:, :], rt_out[:, :], s_a2r, waits=[(s_cc, 3)])
    ctx.wait("sync", (s_wc3, s_wc3.n))
    ab_cast = {}
    for i in range(4):
        st_v = stage[:, i * 2048: i * 2048 + 1024].rearrange("p (c f) -> p c f", c=8)
        st_f = stage[:, i * 2048: i * 2048 + 1024]
        t_d = dma("sync", st_v, ah[8 * i:8 * i + 8, :, :].rearrange("c p f -> p c f"),
                  s_wd3[i])
        ab_cast[i] = ctx.op(
            "vector",
            lambda e, o=at_bf[:, i * 1024:(i + 1) * 1024], s=st_f: e.tensor_copy(o, s),
            waits=[(s_wd3[i], t_d)], inc=s_wc3)
    for i in range(4):
        st_f = stage[:, i * 2048 + 1024: (i + 1) * 2048]
        t_d = dma("sync", st_f, blt_in[:, i * 1024:(i + 1) * 1024], s_wd3[i])
        ab_cast[4 + i] = ctx.op(
            "vector",
            lambda e, o=blt_bf[:, i * 1024:(i + 1) * 1024], s=st_f: e.tensor_copy(o, s),
            waits=[(s_wd3[i], t_d)], inc=s_wc3)
    abprep = s_wc3.n
    t_mm_tok = {}
    for blk in range(2):
        tok = None
        for dc in range(DC):
            tok = ctx.op(
                "tensor",
                lambda e, o=pb[2 + blk][:, :],
                       l=at_bf[:, dc * 128:(dc + 1) * 128],
                       r=x1[:, dc * 1024 + blk * 512: dc * 1024 + (blk + 1) * 512],
                       st=(dc == 0), sp=(dc == DC - 1): e.matmul(o, l, r, start=st, stop=sp),
                waits=([(s_wc3, abprep)] if dc == 0 else ()),
                inc=(s_pe4 if dc == DC - 1 else None))
        t_mm_tok[blk] = tok
    rg_v = rsums_g.rearrange("p (b c) -> p b c", b=2)
    ltok = None
    for dc in range(DC):
        ltok = ctx.op(
            "tensor",
            lambda e, l=rg_v[:, :, dc], r=rwt_sb[:, dc * 8:(dc + 1) * 8]:
                e.matmul(pb[5][0:2, 0:8], l, r, start=(dc == 0), stop=False),
            waits=([(s_a2r, t_rg)] if dc == 0 else ()))
    ltok = ctx.op("tensor",
                  lambda e: e.matmul(pb[5][0:2, 0:8], ones_row[0:1, 0:2], rbias_sb[0:1, 0:8],
                                     start=False, stop=True),
                  inc=s_pe4)
    rl = ctx.op("scalar", lambda e: e.copy(rsm[0:2, 8:16], pb[5][0:2, 0:8]),
                waits=[(s_pe4, ltok)], inc=s_ac4)
    r1b = ctx.op("vector",
                 lambda e: e.reduce_max(rsm[0:2, 1:2], pb[5][0:2, 0:8], axis=AX, negate=True),
                 waits=[(s_pe4, ltok)], inc=s_y4)
    r3 = ctx.op("scalar",
                lambda e: e.activation(rsm[0:2, 16:24], pb[5][0:2, 0:8], AF.Exp,
                                       bias=rsm[0:2, 1:2]),
                waits=[(s_y4, r1b), (s_ac4, rl)], inc=s_ac4, predelay=600)
    ctx.op("vector", lambda e: e.reduce_sum(rsm[0:2, 2:3], rsm[0:2, 16:24], axis=AX),
           waits=[(s_ac4, r3)], predelay=600)
    ctx.op("vector", lambda e: e.nop(cycle_cnt=1000))
    ctx.op("vector", lambda e: e.reciprocal(rsm[0:2, 3:4], rsm[0:2, 2:3]))
    ctx.op("vector", lambda e: e.nop(cycle_cnt=1000))
    r5 = ctx.op("vector",
                lambda e: e.tensor_scalar_mul(rsm[0:2, 4:12], rsm[0:2, 16:24], rsm[0:2, 3:4]),
                inc=s_y4)
    r6 = ctx.op("tensor",
                lambda e: e.transpose(pb[6][0:8, 0:2], rsm[0:2, 4:12], identf[0:2, 0:2]),
                waits=[(s_y4, r5)], inc=s_pe4, predelay=600)
    r7 = ctx.op("scalar", lambda e: e.copy(rsm[0:8, 12:14], pb[6][0:8, 0:2]),
                waits=[(s_pe4, r6)], inc=s_ac4)
    r8 = ctx.op("tensor",
                lambda e: e.matmul(pb[5][:, 0:2], e2_sb[0:8, :], rsm[0:8, 12:14],
                                   start=True, stop=True),
                waits=[(s_ac4, r7)], inc=s_pe4, predelay=600)
    r9 = ctx.op("scalar", lambda e: e.copy(routing_sc[:, :], pb[5][:, 0:2]),
                waits=[(s_pe4, r8)], inc=s_ac4)
    if debug:
        dma("gpsimd", dbg["routing"][:, :], routing_sc[:, :], s_od[0],
            waits=[(s_ac4, r9)])
        dma("gpsimd", dbg["rsums"][:, :], rsums[:, :], s_od[0], waits=[(s_y3, rsums_done)])
        dma("gpsimd", dbg["rsumsg"][:, :], rsums_g[:, :], s_od[0], waits=[(s_a2r, t_rg)])
        dma("gpsimd", dbg["rsm"][:, :], rsm[:, :], s_od[0], waits=[(s_ac4, r9)])
    ts_tok = {}
    for blk in range(2):
        ts_tok[blk] = ctx.op(
            "vector",
            lambda e, o=t_bf[:, blk * 512:(blk + 1) * 512], p=pb[2 + blk],
                   sc=routing_sc[:, blk:blk + 1]:
                e.tensor_scalar_mul(o, p[:, :], sc),
            waits=[(s_pe4, t_mm_tok[blk]), (s_ac4, r9)], inc=s_y4, predelay=400)

    wdma4, wcast4 = {}, {}
    ffn_mm_tok = {}
    lora_tok = {}
    base_ev_tok = {}
    comb_tok = {}
    row_free = s_y4.n
    ln2_done_tok = 0
    wcast4[(-2, 0)] = wcast4[(-2, 1)] = abprep
    wcast4[(-1, 0)] = wcast4[(-1, 1)] = abprep
    fetch_w(wh, 0, 0, wdma4, wcast4, ffn_mm_tok)
    fetch_w(wh, 1, 1, wdma4, wcast4, ffn_mm_tok)
    for blk in range(2):
        for oc in range(DC):
            gi = blk * DC + oc
            wmm = [(s_wc3, wcast4[gi])]
            if gi >= 2:
                wmm.append((s_y4, comb_tok[gi - 2]))
            tok = None
            for dc in range(DC):
                tok = ctx.op(
                    "tensor",
                    lambda e, o=pb[gi % 2][:, :],
                           l=w_bf[gi % 3][:, dc * 128:(dc + 1) * 128],
                           r=x1[:, dc * 1024 + blk * 512: dc * 1024 + (blk + 1) * 512],
                           st=(dc == 0), sp=(dc == DC - 1): e.matmul(o, l, r, start=st, stop=sp),
                    waits=(wmm if dc == 0 else ()),
                    inc=(s_pe4 if dc == DC - 1 else None))
            ffn_mm_tok[gi] = tok
            wl = [(s_y4, ts_tok[blk])]
            if gi >= 1:
                wl.append((s_y4, comb_tok.get(gi - 1, 0)))
            lora_tok[gi] = ctx.op(
                "tensor",
                lambda e, l=blt_bf[:, oc * 128:(oc + 1) * 128],
                       r=t_bf[:, blk * 512:(blk + 1) * 512]:
                    e.matmul(pb[4][:, :], l, r, start=True, stop=True),
                waits=wl, inc=s_pe4)
            wbe = [(s_pe4, ffn_mm_tok[gi])]
            if gi >= 2:
                wbe.append((s_y4, comb_tok[gi - 2]))
            base_ev_tok[gi] = ctx.op(
                "scalar",
                lambda e, o=resid[:, (gi % 2) * 512:(gi % 2 + 1) * 512],
                       p=pb[gi % 2][:, :], bi=btile_sb[:, oc:oc + 1]:
                    e.activation(o, p, AF.Identity, bias=bi),
                waits=wbe, inc=s_ac4)
            wcb = [(s_pe4, lora_tok[gi]), (s_ac3, base_ev_tok[gi])]
            if oc == 0 and blk == 1:
                wcb.append((s_y4, ln2_done_tok))
            ctx.op("vector",
                   lambda e, p=pb[4][:, :], bs=resid[:, (gi % 2) * 512:(gi % 2 + 1) * 512]:
                       e.tensor_add(ytmpf[:, :], p, bs),
                   waits=wcb, predelay=150)
            comb_tok[gi] = yready[(blk, oc)] = ctx.op(
                "vector",
                lambda e, o=attn_all[:, oc * 512:(oc + 1) * 512],
                       x=x1[:, oc * 1024 + blk * 512: oc * 1024 + (blk + 1) * 512]:
                    e.tensor_add(o, x, ytmpf[:, :]),
                inc=s_y4)
            if gi + 2 < 2 * DC:
                fetch_w(wh, (gi + 2) % DC, gi + 2, wdma4, wcast4, ffn_mm_tok)
        _, _, _ = emit_ln(blk, row_free, out_final=True,
                          ysl_fn=lambda oc: attn_all[:, oc * 512:(oc + 1) * 512])
        row_free = s_y4.n
        ln2_done_tok = s_y4.n

    ctx.wait("gpsimd", (s_od[0], s_od[0].n), (s_od[1], s_od[1].n))

    with allow_ici_all_to_all():
        with nc.Block() as block:
            block.sync(lambda e: [f(e) for f in ctx.cmds["sync"]])
            block.tensor(lambda e: [f(e) for f in ctx.cmds["tensor"]])
            block.vector(lambda e: [f(e) for f in ctx.cmds["vector"]])
            block.scalar(lambda e: [f(e) for f in ctx.cmds["scalar"]])
            block.gpsimd(lambda e: [f(e) for f in ctx.cmds["gpsimd"]])
    return nc


def _host_prep(L0_attn_x, L0_attn_Wq, L0_attn_Wk, L0_attn_Wv, L0_attn_Wo,
               L0_ffn_W, L0_ffn_b, L0_ffn_A, L0_ffn_B, L0_ffn_router_W,
               L0_ffn_router_b):
    f32 = np.float32
    x = np.ascontiguousarray(L0_attn_x, dtype=f32)
    xT = np.ascontiguousarray(x.reshape(NT, D).T)
    woh = np.ascontiguousarray(
        np.asarray(L0_attn_Wo, f32).reshape(DC, 128, DC, 128).transpose(0, 3, 2, 1)
    ).reshape(DC, 128, D)
    wh = np.ascontiguousarray(
        np.asarray(L0_ffn_W, f32).reshape(DC, 128, DC, 128).transpose(0, 3, 2, 1)
    ).reshape(DC, 128, D)
    A_all = np.asarray(L0_ffn_A, f32).reshape(128, D)
    ah = np.ascontiguousarray(A_all.reshape(128, DC, 128).transpose(1, 2, 0))
    blt = np.ascontiguousarray(np.asarray(L0_ffn_B, f32).transpose(0, 2, 1).reshape(128, D))
    rwt = np.ascontiguousarray(
        np.asarray(L0_ffn_router_W, f32).T.reshape(DC, 128, 8).transpose(1, 0, 2)
    ).reshape(128, DC * 8)
    rbias = np.ascontiguousarray(np.asarray(L0_ffn_router_b, f32).reshape(1, 8))
    btile = np.ascontiguousarray(np.asarray(L0_ffn_b, f32).reshape(DC, 128).T)
    cosT, sinT = _rope_tables()

    wq = np.asarray(L0_attn_Wq, f32)
    wk = np.asarray(L0_attn_Wk, f32)
    wv = np.asarray(L0_attn_Wv, f32)

    in_maps = []
    for c in range(NCORES):
        wrows = np.concatenate([wq[512 * c:512 * (c + 1)],
                                wk[128 * c:128 * (c + 1)],
                                wv[128 * c:128 * (c + 1)]], axis=0)
        wqkv_h = np.ascontiguousarray(wrows.T).reshape(DC, 128, 768)
        xownT = np.ascontiguousarray(
            np.concatenate([xT[:, 512 * c:512 * c + 512],
                            xT[:, T + 512 * c: T + 512 * c + 512]], axis=1))
        rmask = np.zeros((1, 2 * OWN), f32)
        for blk in range(2):
            for u in range(OWN):
                tpos = 512 * c + u
                if Q_START <= tpos < Q_END:
                    rmask[0, blk * OWN + u] = 1.0 / QMEAN_N
        in_maps.append(dict(
            xT=xT, xownT=xownT, wqkv_h=wqkv_h, woh=woh, wh=wh, ah=ah, blt=blt,
            rwt=rwt, rbias=rbias, btile=btile, rmask=rmask, cosT=cosT, sinT=sinT,
        ))
    return in_maps


_CACHE = {}


def kernel(**inputs):
    in_maps = _host_prep(**inputs)
    if "nc" not in _CACHE:
        _CACHE["nc"] = build_nc(debug=False)
    nc = _CACHE["nc"]
    with allow_ici_all_to_all():
        res = run_bass_kernel_spmd(nc, in_maps, core_ids=list(range(NCORES)))
    out = np.empty((B, T, D), np.float32)
    for c in range(NCORES):
        ot = res.results[c]["out_t"]
        out[0, 512 * c:512 * (c + 1), :] = ot[:, 0:512].T
        out[1, 512 * c:512 * (c + 1), :] = ot[:, 512:1024].T
    return out


# version 13
# speedup vs baseline: 1.0184x; 1.0017x over previous
import os
os.environ.setdefault("TRN_ALLOW_ICI_ALL_TO_ALL", "1")
import math
import numpy as np
import ml_dtypes
import concourse.bass as bass
import concourse.mybir as mybir
from concourse.bass_utils import run_bass_kernel_spmd
from concourse.aot_env import allow_ici_all_to_all

F32 = mybir.dt.float32
BF16 = mybir.dt.bfloat16
AX = mybir.AxisListType.X
ALU = mybir.AluOpType
AF = mybir.ActivationFunctionType

NCORES = 8
B = 2
T = 4096
D = 4096
HD = 128
QH = 4
DC = D // 128
NT = B * T
TB = 256
NTB = NT // TB
TPB = NTB // B
OWN = 512
SM_SCALE = 1.0 / math.sqrt(HD)
SM_SHIFT = -30.0
NEG = -1.0e9
EPS = 1e-5
LORA_SCALE = 2.0
QMEAN_N = 3484.0
Q_START = 611
Q_END = 4095

ENGINES = ("sync", "tensor", "vector", "scalar", "gpsimd")


def _llama3_inv_freq(head_dim=128):
    base = 500000.0
    inv_freq = 1.0 / base ** (np.arange(0, head_dim, 2, dtype=np.float32) / head_dim)
    factor, low_f, high_f, old_ctx = 32.0, 1.0, 4.0, 8192
    low_wl, high_wl = old_ctx / low_f, old_ctx / high_f
    wavelen = 2 * math.pi / inv_freq
    inv_llama = np.where(wavelen > low_wl, inv_freq / factor, inv_freq)
    smooth = (old_ctx / wavelen - low_f) / (high_f - low_f)
    smoothed = (1 - smooth) * inv_llama / factor + smooth * inv_llama
    is_med = ~(wavelen < high_wl) & ~(wavelen > low_wl)
    return np.where(is_med, smoothed, inv_llama).astype(np.float32)


def _rope_tables():
    inv_freq = _llama3_inv_freq(HD)
    pos = np.arange(T, dtype=np.float32)
    freqs = pos[:, None] * inv_freq[None, :]
    emb = np.concatenate([freqs, freqs], axis=-1)
    cos = np.cos(emb).astype(np.float32)
    sin = np.sin(emb).astype(np.float32)
    return np.ascontiguousarray(cos.T), np.ascontiguousarray(sin.T)


class Sem:
    __slots__ = ("h", "name", "n")

    def __init__(self, h, name):
        self.h, self.name, self.n = h, name, 0


class Ctx:
    def __init__(self, nc):
        self.nc = nc
        self.cmds = {e: [] for e in ENGINES}
        self.last_wait = {e: {} for e in ENGINES}
        self._barn = 0

    def sem(self, name):
        return Sem(self.nc.alloc_semaphore(name), name)

    def _dedup(self, engine, waits):
        lw = self.last_wait[engine]
        ws = []
        for s, v in waits:
            if v is None or v <= 0:
                continue
            if lw.get(s.name, -1) < v:
                ws.append((s.h, v))
                lw[s.name] = v
        return ws

    def op(self, engine, fn, waits=(), inc=None, inck=1, predelay=0):
        ws = self._dedup(engine, waits)
        tok = None
        ih = None
        if inc is not None:
            inc.n += inck
            tok = inc.n
            ih = inc.h
        pd = predelay if ws else 0

        def emitter(eng, ws=ws, fn=fn, ih=ih, inck=inck, pd=pd):
            for h, v in ws:
                eng.wait_ge(h, v)
            if pd:
                eng.nop(cycle_cnt=pd)
            instr = fn(eng)
            if ih is not None:
                instr.then_inc(ih, inck)

        self.cmds[engine].append(emitter)
        return tok

    def wait(self, engine, *sem_vals):
        ws = self._dedup(engine, sem_vals)
        if not ws:
            return

        def emitter(eng, ws=ws):
            for h, v in ws:
                eng.wait_ge(h, v)

        self.cmds[engine].append(emitter)

    def barrier(self):
        self._barn += 1
        bar = self.sem(f"bar{self._barn}")
        for e in ENGINES:
            self.op(e, lambda eng: eng.nop(), inc=bar)
        tgt = bar.n
        for e in ENGINES:
            self.wait(e, (bar, tgt))


def build_nc(debug=False, phases=4):
    nc = bass.Bass("TRN2", target_bir_lowering=False)
    ctx = Ctx(nc)

    xT = nc.dram_tensor("xT", [D, NT], F32, kind="ExternalInput").ap()
    xownT = nc.dram_tensor("xownT", [D, 2 * OWN], F32, kind="ExternalInput").ap()
    wqkv_h = nc.dram_tensor("wqkv_h", [DC, 128, 6 * 128], F32, kind="ExternalInput").ap()
    woh = nc.dram_tensor("woh", [DC, 128, D], F32, kind="ExternalInput").ap()
    wh = nc.dram_tensor("wh", [DC, 128, D], F32, kind="ExternalInput").ap()
    ah = nc.dram_tensor("ah", [DC, 128, 128], F32, kind="ExternalInput").ap()
    blt_in = nc.dram_tensor("blt", [128, D], F32, kind="ExternalInput").ap()
    rwt_in = nc.dram_tensor("rwt", [128, DC * 8], F32, kind="ExternalInput").ap()
    rbias_in = nc.dram_tensor("rbias", [1, 8], F32, kind="ExternalInput").ap()
    btile_in = nc.dram_tensor("btile", [128, DC], F32, kind="ExternalInput").ap()
    rmask_in = nc.dram_tensor("rmask", [1, 2 * OWN], F32, kind="ExternalInput").ap()
    cosT = nc.dram_tensor("cosT", [128, T], F32, kind="ExternalInput").ap()
    sinT = nc.dram_tensor("sinT", [128, T], F32, kind="ExternalInput").ap()

    out_t = nc.dram_tensor("out_t", [D, 2 * OWN], F32, kind="ExternalOutput").ap()
    dbg = {}
    if debug:
        dbg["x1"] = nc.dram_tensor("dbg_x1", [128, DC * 2 * OWN], BF16, kind="ExternalOutput").ap()
        dbg["q"] = nc.dram_tensor("dbg_q", [QH * 128, NT], BF16, kind="ExternalOutput").ap()
        dbg["k"] = nc.dram_tensor("dbg_k", [128, NT], BF16, kind="ExternalOutput").ap()
        dbg["v"] = nc.dram_tensor("dbg_v", [128, NT], BF16, kind="ExternalOutput").ap()
        dbg["attn"] = nc.dram_tensor("dbg_attn", [2 * NCORES * 512, 512], BF16, kind="ExternalOutput").ap()
        dbg["routing"] = nc.dram_tensor("dbg_routing", [128, 2], F32, kind="ExternalOutput").ap()
        dbg["rsums"] = nc.dram_tensor("dbg_rsums", [128, 2 * DC], F32, kind="ExternalOutput").ap()
        dbg["rsumsg"] = nc.dram_tensor("dbg_rsumsg", [128, 2 * DC], F32, kind="ExternalOutput").ap()
        dbg["rsm"] = nc.dram_tensor("dbg_rsm", [8, 32], F32, kind="ExternalOutput").ap()

    q_dram = nc.dram_tensor("q_dram", [QH, 128, NT], BF16).ap()
    a2a_in = [nc.dram_tensor(f"a2a_in{b}", [NCORES * 512, 512], BF16).ap() for b in range(B)]
    a2a_out = [nc.dram_tensor(f"a2a_out{b}", [NCORES * 512, 512], BF16).ap() for b in range(B)]
    rt_in = nc.dram_tensor("rt_in", [128, 2 * DC], F32).ap()
    rt_out = nc.dram_tensor("rt_out", [128, 2 * DC], F32).ap()

    tri = np.where(np.arange(128)[None, :] >= np.arange(128)[:, None], 0.0, NEG)
    causal_c = nc.inline_tensor(tri.astype(ml_dtypes.bfloat16), "causal_c").ap()
    ident_bf_c = nc.inline_tensor(np.eye(128, dtype=ml_dtypes.bfloat16), "ident_bf_c").ap()
    ident_f_c = nc.inline_tensor(np.eye(128, dtype=np.float32), "ident_f_c").ap()
    ones_col_c = nc.inline_tensor(np.ones((128, 1), dtype=ml_dtypes.bfloat16), "ones_col_c").ap()
    ones_row_c = nc.inline_tensor(np.ones((1, 128), dtype=np.float32), "ones_row_c").ap()
    e2np = np.zeros((8, 128), dtype=np.float32)
    for e in range(8):
        e2np[e, e * 16:(e + 1) * 16] = LORA_SCALE
    e2_c = nc.inline_tensor(e2np, "e2_c").ap()

    AS = nc.alloc_sbuf_tensor
    causal = AS("causal", [128, 128], BF16).ap()
    identb = AS("identb", [128, 128], BF16).ap()
    identf = AS("identf", [128, 128], F32).ap()
    ones_col = AS("ones_col", [128, 1], BF16).ap()
    ones_row = AS("ones_row", [1, 128], F32).ap()
    e2_sb = AS("e2_sb", [8, 128], F32).ap()
    btile_sb = AS("btile_sb", [128, DC], F32).ap()
    rwt_sb = AS("rwt_sb", [128, DC * 8], F32).ap()
    rbias_sb = AS("rbias_sb", [1, 8], F32).ap()
    rmask_sb = AS("rmask_sb", [1, 2 * OWN], F32).ap()

    rsums = AS("rsums", [128, 2 * DC], F32).ap()
    rsums_g = AS("rsums_g", [128, 2 * DC], F32).ap()
    routing_sc = AS("routing_sc", [128, 2], F32).ap()
    rsm = AS("rsm", [8, 32], F32).ap()
    shift_sb = AS("shift_sb", [128, 1], F32).ap()
    eps_sb = AS("eps_sb", [128, 1], F32).ap()
    row_a = AS("row_a", [1, 512], F32).ap()
    row_b = AS("row_b", [1, 512], F32).ap()
    row_c = AS("row_c", [1, 512], F32).ap()
    stage = AS("stage", [128, 8192], F32).ap()

    AP_ = nc.alloc_psum_tensor
    pb = [AP_(f"pb{i}", [128, 512], F32).ap() for i in range(7)]
    pb7_h = AP_("pb7", [128, 512], F32)
    pb.append(pb7_h.ap())
    pvt = pb7_h.bitcast(BF16).ap()

    g_kv = []
    def _alloc(name, shape, dt, bag):
        g = nc.sbuf_tensor(name, shape, dt)
        ap = g.__enter__().ap()
        bag.append(g)
        return ap

    Karena = _alloc("Karena", [128, NT], BF16, g_kv)
    Vtm = _alloc("Vtm", [128, NT], BF16, g_kv)
    g_p1 = []
    wqkv_bf = _alloc("wqkv_bf", [128, DC * 768], BF16, g_p1)
    cos_bf = _alloc("cos_bf", [128, T], BF16, g_p1)
    sin_bf = _alloc("sin_bf", [128, T], BF16, g_p1)
    x_bf = [_alloc("x_bf0", [128, DC * TB], BF16, g_p1),
            _alloc("x_bf1", [128, DC * TB], BF16, g_p1),
            _alloc("x_bf2", [128, DC * TB], BF16, g_p1)]
    qraw = _alloc("qraw", [128, 2 * TB], BF16, g_p1)
    qrope = _alloc("qrope", [128, 2 * TB], BF16, g_p1)
    rtmp = _alloc("rtmp", [128, TB], BF16, g_p1)
    vraw = _alloc("vraw", [128, 2 * TB], BF16, g_p1)

    def pad_tok(engine, sem, cycles=300):
        return ctx.op(engine, lambda e, c=cycles: e.nop(cycle_cnt=c), inc=sem)

    def dma(engine, out, in_, sem, waits=()):
        return ctx.op(engine, lambda e, o=out, i=in_: e.dma_start(out=o, in_=i),
                      waits=waits, inc=sem, inck=16)

    s_ldc = ctx.sem("s_ldc")
    s_ld0 = ctx.sem("s_ld0")
    s_ld1 = ctx.sem("s_ld1")
    s_cast = ctx.sem("s_cast")

    for (dst, src) in [(causal, causal_c), (identb, ident_bf_c), (identf, ident_f_c),
                       (ones_col, ones_col_c), (ones_row, ones_row_c), (e2_sb, e2_c),
                       (btile_sb, btile_in), (rwt_sb, rwt_in), (rbias_sb, rbias_in),
                       (rmask_sb, rmask_in)]:
        dma("sync", dst[:, :], src[:, :], s_ldc)

    t_cos = dma("sync", stage[:, 0:T], cosT[:, :], s_ld0)
    t_cc = ctx.op("vector", lambda e: e.tensor_copy(cos_bf[:, :], stage[:, 0:T]),
                  waits=[(s_ld0, t_cos)], inc=s_cast)
    t_sin = dma("sync", stage[:, 0:T], sinT[:, :], s_ld0, waits=[(s_cast, t_cc)])
    t_sc = ctx.op("vector", lambda e: e.tensor_copy(sin_bf[:, :], stage[:, 0:T]),
                  waits=[(s_ld0, t_sin)], inc=s_cast)
    ctx.op("vector", lambda e: e.memset(rsums[:, :], 0.0), inc=s_cast)
    ctx.op("vector", lambda e: e.memset(shift_sb[:, :], SM_SHIFT), inc=s_cast)
    ctx.op("vector", lambda e: e.memset(eps_sb[:, :], EPS), inc=s_cast)
    cs_done = s_cast.n

    half_free = {0: cs_done, 1: cs_done}
    for i in range(16):
        hf = i % 2
        st_v = stage[:, hf * 2048: hf * 2048 + 1536].rearrange("p (c f) -> p c f", c=2)
        st_f = stage[:, hf * 2048: hf * 2048 + 1536]
        t_d = dma("sync", st_v, wqkv_h[2 * i:2 * i + 2, :, :].rearrange("c p f -> p c f"),
                  (s_ld0 if hf == 0 else s_ld1), waits=[(s_cast, half_free[hf])])
        half_free[hf] = ctx.op(
            "vector",
            lambda e, o=wqkv_bf[:, i * 1536:(i + 1) * 1536], s=st_f: e.tensor_copy(o, s),
            waits=[((s_ld0 if hf == 0 else s_ld1), t_d)], inc=s_cast)
    wq_cast = s_cast.n

    ctx.wait("tensor", (s_ldc, s_ldc.n))
    ctx.wait("scalar", (s_ldc, s_ldc.n))
    ctx.wait("vector", (s_ldc, s_ldc.n))
    ctx.barrier()

    s_xd = [ctx.sem(f"s_xd{i}") for i in range(4)]
    s_xc = ctx.sem("s_xc")
    s_pe1 = ctx.sem("s_pe1")
    s_ac1 = ctx.sem("s_ac1")
    s_rp1 = ctx.sem("s_rp1")
    s_qd = [ctx.sem("s_qd0"), ctx.sem("s_qd1")]

    xT_v = xT.rearrange("(c p) t -> p c t", p=128)

    xdma_tok, xcast_tok, xfree_tok = {}, {}, {}
    evac_tok, rope_tok, qdma_tok = {}, {}, {}
    vt_tok = {}
    evac_tok_psum = {}

    def prev_qraw(tb, of):
        chain = [0, 2, 4] if of % 2 == 0 else [1, 3]
        i = chain.index(of)
        if i > 0:
            return (tb, chain[i - 1])
        return (tb - 1, chain[-1]) if tb > 0 else None

    def prev_qrope(tb, of):
        chain = [0, 2] if of % 2 == 0 else [1, 3]
        i = chain.index(of)
        if i > 0:
            return (tb, chain[i - 1])
        return (tb - 1, chain[-1]) if tb > 0 else None

    def emit_vtrans(tprev):
        vsl = vraw[:, (tprev % 2) * TB:(tprev % 2 + 1) * TB]
        pslot = (tprev % 2) * 512
        wv = [(s_ac1, evac_tok[(tprev, 5)])]
        if tprev >= 2:
            wv.append((s_ac1, vt_tok[("ac", tprev - 2)]))
        tok = None
        for i in range(TB // 128):
            tok = ctx.op(
                "tensor",
                lambda e, o=pvt[:, pslot + i * 128: pslot + (i + 1) * 128],
                       s=vsl[:, i * 128:(i + 1) * 128]: e.transpose(o, s, identb[:, :]),
                waits=(wv if i == 0 else ()), inc=s_pe1, predelay=(200 if i == 0 else 0))
        vt_tok[("pe", tprev)] = tok
        vt_tok[("ac", tprev)] = ctx.op(
            "scalar",
            lambda e, o=Vtm[:, tprev * TB:(tprev + 1) * TB],
                   s=pvt[:, pslot:pslot + TB]: e.copy(o, s),
            waits=[(s_pe1, tok)], inc=s_ac1)

    for tb in range(NTB):
        par = tb % 3
        for hf in range(4):
            st_v = stage[:, hf * 2048:(hf + 1) * 2048]
            waits = []
            if tb == 0:
                waits.append((s_cast, wq_cast))
            if tb >= 1:
                waits.append((s_xc, xcast_tok[(tb - 1, hf)]))
            if tb >= 3:
                waits.append((s_pe1, xfree_tok[tb - 3]))
            xdma_tok[(tb, hf)] = dma(
                "sync",
                st_v.rearrange("p (c t) -> p c t", c=8),
                xT_v[:, 8 * hf:8 * hf + 8, tb * TB:(tb + 1) * TB],
                s_xd[hf], waits=waits)
            xcast_tok[(tb, hf)] = ctx.op(
                "vector",
                lambda e, o=x_bf[par][:, hf * 8 * TB:(hf + 1) * 8 * TB], s=st_v:
                    e.tensor_copy(o, s),
                waits=[(s_xd[hf], xdma_tok[(tb, hf)])], inc=s_xc)

        for of in range(6):
            gidx = tb * 6 + of
            slot = gidx % 3
            ps = pb[slot]
            w0 = [(s_xc, xcast_tok[(tb, 3)]), (s_cast, wq_cast)]
            if gidx >= 3:
                w0.append((s_ac1, evac_tok_psum[gidx - 3]))
            mm_tok = None
            for dc in range(DC):
                mm_tok = ctx.op(
                    "tensor",
                    lambda e, o=ps[:, 0:TB],
                           l=wqkv_bf[:, dc * 768 + of * 128: dc * 768 + (of + 1) * 128],
                           r=x_bf[par][:, dc * TB:(dc + 1) * TB],
                           st=(dc == 0), sp=(dc == DC - 1): e.matmul(o, l, r, start=st, stop=sp),
                    waits=(w0 if dc == 0 else ()),
                    inc=(s_pe1 if dc == DC - 1 else None))
            if of == 5:
                xfree_tok[tb] = mm_tok

            if of <= 4:
                qpar = of % 2
                if of < 4:
                    rawsl = qraw[:, qpar * TB:(qpar + 1) * TB]
                    ropesl = qrope[:, qpar * TB:(qpar + 1) * TB]
                else:
                    rawsl = qraw[:, 0:TB]
                    ropesl = Karena[:, tb * TB:(tb + 1) * TB]
                w2 = [(s_pe1, mm_tok)]
                pq = prev_qraw(tb, of)
                if pq is not None and pq in rope_tok:
                    w2.append((s_rp1, rope_tok[pq]))
                evac_tok_psum[gidx] = evac_tok[(tb, of)] = ctx.op(
                    "scalar", lambda e, o=rawsl, s=ps[:, 0:TB]: e.copy(o, s),
                    waits=w2, inc=s_ac1, predelay=150)
                pos0 = (tb % TPB) * TB
                cosv = cos_bf[:, pos0:pos0 + TB]
                sinv = sin_bf[:, pos0:pos0 + TB]
                wr = [(s_ac1, evac_tok[(tb, of)])]
                if of < 4:
                    pr = prev_qrope(tb, of)
                    if pr is not None and pr in qdma_tok:
                        wr.append((s_qd[of % 2], qdma_tok[pr]))
                ctx.op("vector",
                       lambda e, o=rtmp[0:64, :], i=rawsl[64:128, :]:
                           e.tensor_scalar_mul(o, i, -1.0), waits=wr, predelay=200)
                ctx.op("vector",
                       lambda e, o=rtmp[64:128, :], i=rawsl[0:64, :]: e.tensor_copy(o, i))
                ctx.op("vector",
                       lambda e, o=ropesl, i=rawsl, c=cosv: e.tensor_mul(o, i, c))
                ctx.op("vector",
                       lambda e, s=sinv: e.tensor_mul(rtmp[:, :], rtmp[:, :], s))
                rope_tok[(tb, of)] = ctx.op(
                    "vector", lambda e, o=ropesl: e.tensor_add(o, o, rtmp[:, :]),
                    inc=s_rp1)
                if of < 4:
                    qdma_tok[(tb, of)] = dma(
                        "gpsimd", q_dram[of, :, tb * TB:(tb + 1) * TB], ropesl,
                        s_qd[of % 2], waits=[(s_rp1, rope_tok[(tb, of)])])
            else:
                vsl = vraw[:, (tb % 2) * TB:(tb % 2 + 1) * TB]
                w2 = [(s_pe1, mm_tok)]
                if tb >= 2:
                    w2.append((s_pe1, vt_tok[("pe", tb - 2)]))
                evac_tok_psum[gidx] = evac_tok[(tb, 5)] = ctx.op(
                    "scalar", lambda e, o=vsl, s=ps[:, 0:TB]: e.copy(o, s),
                    waits=w2, inc=s_ac1, predelay=150)
        if tb >= 1:
            emit_vtrans(tb - 1)
    emit_vtrans(NTB - 1)

    ctx.barrier()

    def emit_dumps_and_finish(s_fin):
        dma("gpsimd", dbg["k"][:, :], Karena[:, :], s_fin)
        dma("gpsimd", dbg["v"][:, :], Vtm[:, :], s_fin)
        dma("gpsimd", dbg["q"].rearrange("(h p) t -> h p t", p=128), q_dram[:, :, :], s_fin)
        ctx.wait("gpsimd", (s_fin, s_fin.n))
        with allow_ici_all_to_all():
            with nc.Block() as block:
                block.sync(lambda e: [f(e) for f in ctx.cmds["sync"]])
                block.tensor(lambda e: [f(e) for f in ctx.cmds["tensor"]])
                block.vector(lambda e: [f(e) for f in ctx.cmds["vector"]])
                block.scalar(lambda e: [f(e) for f in ctx.cmds["scalar"]])
                block.gpsimd(lambda e: [f(e) for f in ctx.cmds["gpsimd"]])
        return nc

    if phases == 1:
        return emit_dumps_and_finish(ctx.sem("s_fin"))

    for g in reversed(g_p1):
        g.__exit__(None, None, None)
    g_p1 = []
    def _alloc_r(name, shape, dt, bag):
        g = nc.sbuf_tensor(name, shape, dt, side="right")
        ap = g.__enter__().ap()
        bag.append(g)
        return ap
    g_w = []
    w_bf = [_alloc_r("w_bf0", [128, 4096], BF16, g_w), _alloc_r("w_bf1", [128, 4096], BF16, g_w),
            _alloc_r("w_bf2", [128, 4096], BF16, g_w)]
    g_p2 = []
    qu = [_alloc("qu0", [128, T], BF16, g_p2), _alloc("qu1", [128, T], BF16, g_p2)]
    pt = _alloc("pt", [128, 5 * 512], BF16, g_p2)
    a2at = _alloc("a2at", [128, 2 * 512], BF16, g_p2)
    rbc = _alloc("rbc", [128, 512], F32, g_p2)
    recipr = _alloc("recipr", [1, 512], F32, g_p2)

    s_qld = [ctx.sem("s_qld0"), ctx.sem("s_qld1")]
    s_a2r = ctx.sem("s_a2r")
    s_wd3 = [ctx.sem(f"s_wd3{i}") for i in range(4)]
    s_wc3 = ctx.sem("s_wc3")
    s_pe3 = ctx.sem("s_pe3")
    s_st = ctx.sem("s_st")
    s_msk = ctx.sem("s_msk")
    s_exp = ctx.sem("s_exp")
    s_av = ctx.sem("s_av")
    s_nrm = ctx.sem("s_nrm")
    s_bcst = ctx.sem("s_bcst")
    s_bcc = ctx.sem("s_bcc")
    s_a2d = [ctx.sem("s_a2d0"), ctx.sem("s_a2d1")]
    s_rtd = ctx.sem("s_rtd")
    s_cc = ctx.sem("s_cc")

    def fetch_w(src3d, oc, gi, wdma_tok, wcast_tok, mm_done_tok):
        wpar = gi % 3
        for hf in range(2):
            sidx = (gi * 2 + hf) % 4
            st_f = stage[:, sidx * 2048:(sidx + 1) * 2048]
            wd = []
            if (gi - 2, hf) in wcast_tok:
                wd.append((s_wc3, wcast_tok[(gi - 2, hf)]))
            wdma_tok[(gi, hf)] = dma("sync", st_f, src3d[oc, :, hf * 2048:(hf + 1) * 2048],
                                     s_wd3[sidx], waits=wd)
            wc = [(s_wd3[sidx], wdma_tok[(gi, hf)])]
            if hf == 0 and (gi - 3) in mm_done_tok:
                wc.append((s_pe3, mm_done_tok[gi - 3]))
            wcast_tok[(gi, hf)] = ctx.op(
                "vector",
                lambda e, o=w_bf[wpar][:, hf * 2048:(hf + 1) * 2048], s=st_f: e.tensor_copy(o, s),
                waits=wc, inc=s_wc3)
        wcast_tok[gi] = wcast_tok[(gi, 1)]

    st_tok, exp_tok, av_tok = {}, {}, {}
    unit_last_st = {}
    slot_bc_guard = {}
    avb_free = {}
    sum_free = 0
    kchunk = 0
    nblk = 0

    for u in range(2 * QH):
        b, h = u // QH, u % QH
        par = u % 2
        wq_ = [(s_qd[0], s_qd[0].n), (s_qd[1], s_qd[1].n)]
        if u < 2:
            wq_ += [(s_pe1, s_pe1.n), (s_rp1, s_rp1.n)]
        if u >= 2:
            wq_.append((s_st, unit_last_st[u - 2]))
        t_q = dma("sync", qu[par][:, :], q_dram[h, :, b * T:(b + 1) * T],
                  s_qld[par], waits=wq_)
        descs = []
        for qb in range(8):
            nch = 4 * qb + 4
            for c in range(nch):
                sufs = 0 if c < 4 * qb else (c - 4 * qb) * 128
                descs.append((qb, c, sufs, c >= 4 * qb, c == nch - 1))
        N = len(descs)
        LA = 4
        NS = 5
        SLOTS = (0, 1, 2, 3, 7)
        k0 = kchunk

        def emit_st(i):
            qb, c, sufs, diag, last = descs[i]
            kk = k0 + i
            slot = SLOTS[kk % NS]
            wst = [(s_qld[par], t_q)]
            if kk >= NS:
                wst.append((s_exp, exp_tok[kk - NS]))
            if slot_bc_guard.get(slot):
                wst.append((s_bcc, slot_bc_guard.pop(slot)))
            ctx.op(
                "tensor",
                lambda e, o=pb[slot][:, sufs:512],
                       l=Karena[:, b * T + c * 128: b * T + (c + 1) * 128],
                       r=qu[par][:, qb * 512 + sufs:(qb + 1) * 512],
                       sp=(not diag): e.matmul(o, l, r, start=True, stop=sp),
                waits=wst, inc=(None if diag else s_st))
            if diag:
                ctx.op(
                    "tensor",
                    lambda e, o=pb[slot][:, sufs:sufs + 128]:
                        e.matmul(o, identb[:, :], causal[:, :], start=False, stop=True),
                    inc=s_st)
            st_tok[kk] = s_st.n
            unit_last_st[u] = s_st.n

        for j in range(min(LA, N)):
            emit_st(j)
        for i in range(N):
            qb, c, sufs, diag, last = descs[i]
            kk = k0 + i
            slot = SLOTS[kk % NS]
            nch = 4 * qb + 4
            avb = pb[4 + nblk % 2]
            we = [(s_st, st_tok[kk])]
            if kk >= NS:
                we.append((s_av, av_tok[kk - NS]))
            exp_tok[kk] = ctx.op(
                "scalar",
                lambda e, o=pt[:, (kk % NS) * 512 + sufs:(kk % NS + 1) * 512],
                       s=pb[slot][:, sufs:512]:
                    e.activation(o, s, AF.Exp, bias=shift_sb[:, 0:1], scale=SM_SCALE),
                waits=we, inc=s_exp, predelay=(150 if diag else 0))
            wav = [(s_exp, exp_tok[kk])]
            if c == 0:
                wav.append((s_nrm, avb_free.get(nblk % 2, 0)))
            ctx.op(
                "tensor",
                lambda e, o=avb[:, sufs:512],
                       l=Vtm[:, (b * 32 + c) * 128:(b * 32 + c + 1) * 128],
                       r=pt[:, (kk % NS) * 512 + sufs:(kk % NS + 1) * 512],
                       st=(c == 0), sp=(c == nch - 1): e.matmul(o, l, r, start=st, stop=sp),
                waits=wav)
            wsum = []
            if c == 0:
                wsum.append((s_nrm, sum_free))
            av_tok[kk] = ctx.op(
                "tensor",
                lambda e, o=pb[6][0:1, sufs:512],
                       r=pt[:, (kk % NS) * 512 + sufs:(kk % NS + 1) * 512],
                       st=(c == 0), sp=(c == nch - 1):
                    e.matmul(o, ones_col[:, :], r, start=st, stop=sp),
                waits=wsum, inc=s_av)
            if i + LA < N:
                emit_st(i + LA)
            if last:
                rc_tok = ctx.op(
                    "vector", lambda e: e.reciprocal(recipr[:, :], pb[6][0:1, :]),
                    waits=[(s_av, av_tok[kk]), (s_bcst, s_bcst.n)], inc=s_nrm,
                    predelay=150)
                sum_free = rc_tok
                bslot = slot
                bc_tok = ctx.op(
                    "tensor",
                    lambda e, o=pb[bslot][:, :]: e.matmul(o, ones_row[0:1, :], recipr[0:1, :],
                                                          start=True, stop=True),
                    waits=[(s_nrm, rc_tok), (s_exp, exp_tok[kk]), (s_bcc, s_bcc.n)],
                    inc=s_bcst, predelay=300)
                slot_bc_guard[bslot] = None
                wbcc = [(s_bcst, bc_tok)]
                if nblk >= 1:
                    wbcc.append((s_nrm, s_nrm.n))
                bcc_tok = ctx.op(
                    "scalar", lambda e, o=pb[bslot]: e.copy(rbc[:, :], o[:, :]),
                    waits=wbcc, inc=s_bcc)
                slot_bc_guard[bslot] = bcc_tok
                asl = a2at[:, (nblk % 2) * 512:(nblk % 2 + 1) * 512]
                wn = [(s_bcc, bcc_tok)]
                if nblk >= 2:
                    wn.append((s_a2d[nblk % 2], 16 * ((nblk - 2) // 2 + 1)))
                n_tok = ctx.op(
                    "vector", lambda e, o=asl, a=avb: e.tensor_mul(o, a[:, :], rbc[:, :]),
                    waits=wn, inc=s_nrm, predelay=300)
                avb_free[nblk % 2] = n_tok
                dma("gpsimd", a2a_in[b][qb * 512 + h * 128: qb * 512 + (h + 1) * 128, :], asl,
                    s_a2d[nblk % 2], waits=[(s_nrm, n_tok)])
                nblk += 1
        kchunk += N
        if h == QH - 1:
            ctx.wait("gpsimd", (s_a2d[0], s_a2d[0].n), (s_a2d[1], s_a2d[1].n))
            ctx.op("gpsimd",
                   lambda e, i=a2a_in[b], o=a2a_out[b]: e.collective_compute(
                       "AllToAll", ALU.bypass,
                       replica_groups=[list(range(NCORES))],
                       ins=[i.opt()], outs=[o.opt()]),
                   inc=s_cc)

    wdma3, wcast3 = {}, {}
    ctx.wait("sync", (s_xc, s_xc.n))
    fetch_w(woh, 0, 0, wdma3, wcast3, {})
    fetch_w(woh, 1, 1, wdma3, wcast3, {})

    ctx.barrier()
    if debug:
        s_dbg = ctx.sem("s_dbg")
        dma("gpsimd", dbg["k"][:, :], Karena[:, :], s_dbg)
        dma("gpsimd", dbg["v"][:, :], Vtm[:, :], s_dbg)
        dma("gpsimd", dbg["q"].rearrange("(h p) t -> h p t", p=128), q_dram[:, :, :], s_dbg)
        for b_ in range(B):
            dma("gpsimd", dbg["attn"][b_ * NCORES * 512:(b_ + 1) * NCORES * 512, :],
                a2a_out[b_][:, :], s_dbg, waits=[(s_cc, b_ + 1)])
        ctx.wait("gpsimd", (s_dbg, s_dbg.n))
        ctx.barrier()
    for g in reversed(g_p2):
        g.__exit__(None, None, None)
    for g in reversed(g_kv):
        g.__exit__(None, None, None)

    g_p3 = []
    x1 = _alloc("x1", [128, DC * 2 * OWN], BF16, g_p3)
    attn_all = _alloc("attn_all", [128, DC * 512], BF16, g_p3)
    resid = _alloc("resid", [128, 2 * 512], F32, g_p3)
    sqtile = _alloc("sqtile", [128, 2 * 512], BF16, g_p3)
    ytmpf = _alloc("ytmpf", [128, 512], F32, g_p3)
    bcab = _alloc("bcab", [128, 2 * 512], F32, g_p3)
    mask_bc = _alloc("mask_bc", [128, 512], F32, g_p3)
    t_bf = _alloc("t_bf", [128, 2 * 512], BF16, g_p3)
    at_bf = _alloc("at_bf", [128, DC * 128], BF16, g_p3)
    blt_bf = _alloc("blt_bf", [128, D], BF16, g_p3)
    osl = _alloc("osl", [128, 2 * 512], F32, g_p3)

    s_y3 = ctx.sem("s_y3")
    s_ac3 = ctx.sem("s_ac3")
    s_rd3 = [ctx.sem("s_rd30"), ctx.sem("s_rd31")]
    s_od = [ctx.sem("s_od0"), ctx.sem("s_od1")]



    def emit_ln(blk, bc_wait_tok, out_final, ysl_fn=None):
        sq_tok = {}
        stats_tok = {}
        if ysl_fn is None:
            ysl_fn = lambda oc: x1[:, oc * 1024 + blk * 512: oc * 1024 + (blk + 1) * 512]
        for oc in range(DC):
            ysl = ysl_fn(oc)
            wsq = [(s_y3, yready[(blk, oc)])]
            if oc >= 2:
                wsq.append((s_pe3, stats_tok[oc - 2]))
            sq_tok[oc] = ctx.op(
                "scalar",
                lambda e, o=sqtile[:, (oc % 2) * 512:(oc % 2 + 1) * 512], s=ysl:
                    e.activation(o, s, AF.Square),
                waits=wsq, inc=s_ac3, predelay=150)
            wst = [(s_y3, yready[(blk, oc)])]
            if oc == 0:
                wst.append((s_y3, bc_wait_tok))
            ctx.op("tensor",
                   lambda e, o=pb[2][0:1, :], r=ysl, st=(oc == 0), sp=(oc == DC - 1):
                       e.matmul(o, ones_col[:, :], r, start=st, stop=sp),
                   waits=wst)
            stats_tok[oc] = ctx.op(
                "tensor",
                lambda e, o=pb[3][0:1, :], r=sqtile[:, (oc % 2) * 512:(oc % 2 + 1) * 512],
                       st=(oc == 0), sp=(oc == DC - 1):
                    e.matmul(o, ones_col[:, :], r, start=st, stop=sp),
                waits=[(s_ac3, sq_tok[oc])], inc=s_pe3, predelay=150)
        t1 = ctx.op("vector", lambda e: e.tensor_scalar_mul(row_a[:, :], pb[2][0:1, :], 1.0 / D),
                    waits=[(s_pe3, stats_tok[DC - 1])], inc=s_y3, predelay=150)
        ctx.op("vector", lambda e: e.tensor_scalar_mul(row_b[:, :], pb[3][0:1, :], 1.0 / D))
        ctx.op("vector", lambda e: e.tensor_mul(row_c[:, :], row_a[:, :], row_a[:, :]))
        t2 = ctx.op("vector", lambda e: e.tensor_sub(row_b[:, :], row_b[:, :], row_c[:, :]),
                    inc=s_y3)
        ctx.op("scalar", lambda e: e.activation(row_c[:, :], row_b[:, :], AF.Ln, bias=eps_sb[0:1, 0:1]),
               waits=[(s_y3, t2)], inc=s_ac3, predelay=300)
        t4 = ctx.op("scalar", lambda e: e.activation(row_b[:, :], row_c[:, :], AF.Exp, scale=-0.5),
                    inc=s_ac3)
        t5 = ctx.op("vector",
                    lambda e: e.scalar_tensor_tensor(row_c[:, :], row_a[:, :], -1.0, row_b[:, :],
                                                     ALU.mult, ALU.mult),
                    waits=[(s_ac3, t4)], inc=s_y3, predelay=300)
        tba = ctx.op("tensor", lambda e: e.matmul(pb[4][:, :], ones_row[0:1, :], row_b[0:1, :],
                                                  start=True, stop=True),
                     waits=[(s_y3, t5), (s_ac3, s_ac3.n), (s_y3, s_y3.n)], inc=s_pe3,
                     predelay=300)
        tbc = ctx.op("tensor", lambda e: e.matmul(pb[5][:, :], ones_row[0:1, :], row_c[0:1, :],
                                                  start=True, stop=True), inc=s_pe3)
        tca = ctx.op("scalar", lambda e: e.copy(bcab[:, 0:512], pb[4][:, :]),
                     waits=[(s_pe3, tbc)], inc=s_ac3)
        tcc = ctx.op("scalar", lambda e: e.copy(bcab[:, 512:1024], pb[5][:, :]), inc=s_ac3)
        norm_toks = {}
        for oc in range(DC):
            ysl = ysl_fn(oc)
            if out_final:
                odst = osl[:, (oc % 2) * 512:(oc % 2 + 1) * 512]
                wn = [(s_ac3, tcc)]
                if oc >= 2:
                    wn.append((s_od[oc % 2], od_tok[(blk, oc - 2)]))
                elif blk >= 1:
                    wn.append((s_od[oc % 2], od_tok[(blk - 1, DC - 2 + oc)]))
                ctx.op("vector", lambda e, y=ysl, o=ytmpf: e.tensor_mul(o[:, :], y, bcab[:, 0:512]),
                       waits=wn, predelay=300)
                norm_toks[oc] = ctx.op(
                    "vector", lambda e, o=odst: e.tensor_add(o, ytmpf[:, :], bcab[:, 512:1024]),
                    inc=s_y3)
                od_tok[(blk, oc)] = dma(
                    "gpsimd", out_t[oc * 128:(oc + 1) * 128, blk * 512:(blk + 1) * 512],
                    odst, s_od[oc % 2], waits=[(s_y3, norm_toks[oc])])
            else:
                ctx.op("vector", lambda e, y=ysl: e.tensor_mul(ytmpf[:, :], y, bcab[:, 0:512]),
                       waits=[(s_ac3, tcc)], predelay=300)
                norm_toks[oc] = ctx.op(
                    "vector", lambda e, y=ysl: e.tensor_add(y, ytmpf[:, :], bcab[:, 512:1024]),
                    inc=s_y3)
        return tca, tcc, norm_toks

    yready = {}
    od_tok = {}
    wo_mm_tok = {}
    wo_y_tok = {}
    x1row_free = 0
    stt_done = 0
    for blk in range(2):
        ctx.wait("sync", (s_cc, blk + 1))
        for s in range(NCORES):
            wr = [(s_pe3, wo_mm_tok[DC - 1])] if (blk == 1 and s == 0) else []
            rb_tok = dma("sync",
                         attn_all[:, s * 2048:(s + 1) * 2048].rearrange(
                             "p (h t) -> p h t", h=4),
                         a2a_out[blk][s * 512:(s + 1) * 512, :].rearrange(
                             "(h p) t -> p h t", p=128),
                         s_a2r, waits=wr)
        for oc in range(DC):
            gi = blk * DC + oc
            rwt_ = []
            if gi >= 2:
                rwt_.append((s_y3, yready[(blk, oc - 2)] if oc >= 2 else yready[(blk - 1, DC - 2 + oc)]))
            rd_tok = dma("gpsimd", resid[:, (gi % 2) * 512:(gi % 2 + 1) * 512],
                         xownT[oc * 128:(oc + 1) * 128, blk * 512:(blk + 1) * 512],
                         s_rd3[gi % 2], waits=rwt_)
            wmm = [(s_a2r, rb_tok), (s_wc3, wcast3[gi])]
            if gi >= 2:
                wmm.append((s_y3, wo_y_tok[gi - 2]))
            tok = None
            for fc in range(DC):
                tok = ctx.op(
                    "tensor",
                    lambda e, o=pb[gi % 2][:, :],
                           l=w_bf[gi % 3][:, fc * 128:(fc + 1) * 128],
                           r=attn_all[:, fc * 512:(fc + 1) * 512],
                           st=(fc == 0), sp=(fc == DC - 1): e.matmul(o, l, r, start=st, stop=sp),
                    waits=(wmm if fc == 0 else ()),
                    inc=(s_pe3 if fc == DC - 1 else None))
            wo_mm_tok[gi] = tok
            wo_y_tok[gi] = yready[(blk, oc)] = ctx.op(
                "vector",
                lambda e, o=x1[:, oc * 1024 + blk * 512: oc * 1024 + (blk + 1) * 512],
                       p=pb[gi % 2][:, :], r=resid[:, (gi % 2) * 512:(gi % 2 + 1) * 512]:
                    e.tensor_add(o, p, r),
                waits=[(s_pe3, tok), (s_rd3[gi % 2], rd_tok)], inc=s_y3, predelay=150)
            if gi + 2 < 2 * DC:
                fetch_w(woh, (gi + 2) % DC, gi + 2, wdma3, wcast3, wo_mm_tok)
        tca, tcc, norm_toks = emit_ln(blk, x1row_free, out_final=False)
        x1row_free = s_y3.n
        tmb = ctx.op("tensor",
                     lambda e, r=rmask_sb[0:1, blk * 512:(blk + 1) * 512]:
                         e.matmul(pb[4][:, :], ones_row[0:1, :], r, start=True, stop=True),
                     waits=[(s_ac3, tca), (s_y3, stt_done)], inc=s_pe3)
        tmc = ctx.op("scalar",
                     lambda e: e.copy(mask_bc[:, :], pb[4][:, :]),
                     waits=[(s_pe3, tmb)], inc=s_ac3)
        for oc in range(DC):
            ysl = x1[:, oc * 1024 + blk * 512: oc * 1024 + (blk + 1) * 512]
            ctx.op("vector",
                   lambda e, y=ysl, o=rsums[:, blk * DC + oc: blk * DC + oc + 1],
                          m=mask_bc[:, :]:
                       e.scalar_tensor_tensor(ytmpf[:, :], y, 1.0, m, ALU.mult, ALU.mult,
                                              accum_out=o),
                   waits=[(s_y3, norm_toks[oc]), (s_ac3, tmc)],
                   predelay=300,
                   inc=(s_y3 if oc == DC - 1 else None))
        stt_done = s_y3.n
    rsums_done = s_y3.n

    t_ri = dma("gpsimd", rt_in[:, :], rsums[:, :], s_rtd, waits=[(s_y3, rsums_done)])
    ctx.wait("gpsimd", (s_rtd, t_ri))
    ctx.op("gpsimd",
           lambda e: e.collective_compute(
               "AllReduce", ALU.add, replica_groups=[list(range(NCORES))],
               ins=[rt_in.opt()], outs=[rt_out.opt()]),
           inc=s_cc)
    if debug:
        t_dx = dma("gpsimd", dbg["x1"][:, :], x1[:, :], s_od[0], waits=[(s_y3, rsums_done)])
        ctx.wait("gpsimd", (s_od[0], t_dx))

    ctx.barrier()

    s_pe4 = s_pe3
    s_y4 = s_y3
    s_ac4 = s_ac3

    t_rg = dma("gpsimd", rsums_g[:, :], rt_out[:, :], s_a2r, waits=[(s_cc, 3)])
    ctx.wait("sync", (s_wc3, s_wc3.n))
    ab_cast = {}
    for i in range(4):
        st_v = stage[:, i * 2048: i * 2048 + 1024].rearrange("p (c f) -> p c f", c=8)
        st_f = stage[:, i * 2048: i * 2048 + 1024]
        t_d = dma("sync", st_v, ah[8 * i:8 * i + 8, :, :].rearrange("c p f -> p c f"),
                  s_wd3[i])
        ab_cast[i] = ctx.op(
            "vector",
            lambda e, o=at_bf[:, i * 1024:(i + 1) * 1024], s=st_f: e.tensor_copy(o, s),
            waits=[(s_wd3[i], t_d)], inc=s_wc3)
    for i in range(4):
        st_f = stage[:, i * 2048 + 1024: (i + 1) * 2048]
        t_d = dma("sync", st_f, blt_in[:, i * 1024:(i + 1) * 1024], s_wd3[i])
        ab_cast[4 + i] = ctx.op(
            "vector",
            lambda e, o=blt_bf[:, i * 1024:(i + 1) * 1024], s=st_f: e.tensor_copy(o, s),
            waits=[(s_wd3[i], t_d)], inc=s_wc3)
    abprep = s_wc3.n
    t_mm_tok = {}
    for blk in range(2):
        tok = None
        for dc in range(DC):
            tok = ctx.op(
                "tensor",
                lambda e, o=pb[2 + blk][:, :],
                       l=at_bf[:, dc * 128:(dc + 1) * 128],
                       r=x1[:, dc * 1024 + blk * 512: dc * 1024 + (blk + 1) * 512],
                       st=(dc == 0), sp=(dc == DC - 1): e.matmul(o, l, r, start=st, stop=sp),
                waits=([(s_wc3, abprep)] if dc == 0 else ()),
                inc=(s_pe4 if dc == DC - 1 else None))
        t_mm_tok[blk] = tok
    rg_v = rsums_g.rearrange("p (b c) -> p b c", b=2)
    ltok = None
    for dc in range(DC):
        ltok = ctx.op(
            "tensor",
            lambda e, l=rg_v[:, :, dc], r=rwt_sb[:, dc * 8:(dc + 1) * 8]:
                e.matmul(pb[5][0:2, 0:8], l, r, start=(dc == 0), stop=False),
            waits=([(s_a2r, t_rg)] if dc == 0 else ()))
    ltok = ctx.op("tensor",
                  lambda e: e.matmul(pb[5][0:2, 0:8], ones_row[0:1, 0:2], rbias_sb[0:1, 0:8],
                                     start=False, stop=True),
                  inc=s_pe4)
    rl = ctx.op("scalar", lambda e: e.copy(rsm[0:2, 8:16], pb[5][0:2, 0:8]),
                waits=[(s_pe4, ltok)], inc=s_ac4)
    r1b = ctx.op("vector",
                 lambda e: e.reduce_max(rsm[0:2, 1:2], pb[5][0:2, 0:8], axis=AX, negate=True),
                 waits=[(s_pe4, ltok)], inc=s_y4)
    r3 = ctx.op("scalar",
                lambda e: e.activation(rsm[0:2, 16:24], pb[5][0:2, 0:8], AF.Exp,
                                       bias=rsm[0:2, 1:2]),
                waits=[(s_y4, r1b), (s_ac4, rl)], inc=s_ac4, predelay=600)
    ctx.op("vector", lambda e: e.reduce_sum(rsm[0:2, 2:3], rsm[0:2, 16:24], axis=AX),
           waits=[(s_ac4, r3)], predelay=600)
    ctx.op("vector", lambda e: e.nop(cycle_cnt=1000))
    ctx.op("vector", lambda e: e.reciprocal(rsm[0:2, 3:4], rsm[0:2, 2:3]))
    ctx.op("vector", lambda e: e.nop(cycle_cnt=1000))
    r5 = ctx.op("vector",
                lambda e: e.tensor_scalar_mul(rsm[0:2, 4:12], rsm[0:2, 16:24], rsm[0:2, 3:4]),
                inc=s_y4)
    r6 = ctx.op("tensor",
                lambda e: e.transpose(pb[6][0:8, 0:2], rsm[0:2, 4:12], identf[0:2, 0:2]),
                waits=[(s_y4, r5)], inc=s_pe4, predelay=600)
    r7 = ctx.op("scalar", lambda e: e.copy(rsm[0:8, 12:14], pb[6][0:8, 0:2]),
                waits=[(s_pe4, r6)], inc=s_ac4)
    r8 = ctx.op("tensor",
                lambda e: e.matmul(pb[5][:, 0:2], e2_sb[0:8, :], rsm[0:8, 12:14],
                                   start=True, stop=True),
                waits=[(s_ac4, r7)], inc=s_pe4, predelay=600)
    r9 = ctx.op("scalar", lambda e: e.copy(routing_sc[:, :], pb[5][:, 0:2]),
                waits=[(s_pe4, r8)], inc=s_ac4)
    if debug:
        dma("gpsimd", dbg["routing"][:, :], routing_sc[:, :], s_od[0],
            waits=[(s_ac4, r9)])
        dma("gpsimd", dbg["rsums"][:, :], rsums[:, :], s_od[0], waits=[(s_y3, rsums_done)])
        dma("gpsimd", dbg["rsumsg"][:, :], rsums_g[:, :], s_od[0], waits=[(s_a2r, t_rg)])
        dma("gpsimd", dbg["rsm"][:, :], rsm[:, :], s_od[0], waits=[(s_ac4, r9)])
    ts_tok = {}
    for blk in range(2):
        ts_tok[blk] = ctx.op(
            "vector",
            lambda e, o=t_bf[:, blk * 512:(blk + 1) * 512], p=pb[2 + blk],
                   sc=routing_sc[:, blk:blk + 1]:
                e.tensor_scalar_mul(o, p[:, :], sc),
            waits=[(s_pe4, t_mm_tok[blk]), (s_ac4, r9)], inc=s_y4, predelay=400)

    wdma4, wcast4 = {}, {}
    ffn_mm_tok = {}
    lora_tok = {}
    base_ev_tok = {}
    comb_tok = {}
    row_free = s_y4.n
    ln2_done_tok = 0
    wcast4[(-2, 0)] = wcast4[(-2, 1)] = abprep
    wcast4[(-1, 0)] = wcast4[(-1, 1)] = abprep
    fetch_w(wh, 0, 0, wdma4, wcast4, ffn_mm_tok)
    fetch_w(wh, 1, 1, wdma4, wcast4, ffn_mm_tok)
    for blk in range(2):
        for oc in range(DC):
            gi = blk * DC + oc
            wmm = [(s_wc3, wcast4[gi])]
            if gi >= 2:
                wmm.append((s_y4, comb_tok[gi - 2]))
            tok = None
            for dc in range(DC):
                tok = ctx.op(
                    "tensor",
                    lambda e, o=pb[gi % 2][:, :],
                           l=w_bf[gi % 3][:, dc * 128:(dc + 1) * 128],
                           r=x1[:, dc * 1024 + blk * 512: dc * 1024 + (blk + 1) * 512],
                           st=(dc == 0), sp=(dc == DC - 1): e.matmul(o, l, r, start=st, stop=sp),
                    waits=(wmm if dc == 0 else ()),
                    inc=(s_pe4 if dc == DC - 1 else None))
            ffn_mm_tok[gi] = tok
            wl = [(s_y4, ts_tok[blk])]
            if gi >= 1:
                wl.append((s_y4, comb_tok.get(gi - 1, 0)))
            lora_tok[gi] = ctx.op(
                "tensor",
                lambda e, l=blt_bf[:, oc * 128:(oc + 1) * 128],
                       r=t_bf[:, blk * 512:(blk + 1) * 512]:
                    e.matmul(pb[4][:, :], l, r, start=True, stop=True),
                waits=wl, inc=s_pe4)
            wbe = [(s_pe4, ffn_mm_tok[gi])]
            if gi >= 2:
                wbe.append((s_y4, comb_tok[gi - 2]))
            base_ev_tok[gi] = ctx.op(
                "scalar",
                lambda e, o=resid[:, (gi % 2) * 512:(gi % 2 + 1) * 512],
                       p=pb[gi % 2][:, :], bi=btile_sb[:, oc:oc + 1]:
                    e.activation(o, p, AF.Identity, bias=bi),
                waits=wbe, inc=s_ac4)
            wcb = [(s_pe4, lora_tok[gi]), (s_ac3, base_ev_tok[gi])]
            if oc == 0 and blk == 1:
                wcb.append((s_y4, ln2_done_tok))
            ctx.op("vector",
                   lambda e, p=pb[4][:, :], bs=resid[:, (gi % 2) * 512:(gi % 2 + 1) * 512]:
                       e.tensor_add(ytmpf[:, :], p, bs),
                   waits=wcb, predelay=150)
            comb_tok[gi] = yready[(blk, oc)] = ctx.op(
                "vector",
                lambda e, o=attn_all[:, oc * 512:(oc + 1) * 512],
                       x=x1[:, oc * 1024 + blk * 512: oc * 1024 + (blk + 1) * 512]:
                    e.tensor_add(o, x, ytmpf[:, :]),
                inc=s_y4)
            if gi + 2 < 2 * DC:
                fetch_w(wh, (gi + 2) % DC, gi + 2, wdma4, wcast4, ffn_mm_tok)
        _, _, _ = emit_ln(blk, row_free, out_final=True,
                          ysl_fn=lambda oc: attn_all[:, oc * 512:(oc + 1) * 512])
        row_free = s_y4.n
        ln2_done_tok = s_y4.n

    ctx.wait("gpsimd", (s_od[0], s_od[0].n), (s_od[1], s_od[1].n))

    with allow_ici_all_to_all():
        with nc.Block() as block:
            block.sync(lambda e: [f(e) for f in ctx.cmds["sync"]])
            block.tensor(lambda e: [f(e) for f in ctx.cmds["tensor"]])
            block.vector(lambda e: [f(e) for f in ctx.cmds["vector"]])
            block.scalar(lambda e: [f(e) for f in ctx.cmds["scalar"]])
            block.gpsimd(lambda e: [f(e) for f in ctx.cmds["gpsimd"]])
    return nc


def _host_prep(L0_attn_x, L0_attn_Wq, L0_attn_Wk, L0_attn_Wv, L0_attn_Wo,
               L0_ffn_W, L0_ffn_b, L0_ffn_A, L0_ffn_B, L0_ffn_router_W,
               L0_ffn_router_b):
    f32 = np.float32
    x = np.ascontiguousarray(L0_attn_x, dtype=f32)
    xT = np.ascontiguousarray(x.reshape(NT, D).T)
    woh = np.ascontiguousarray(
        np.asarray(L0_attn_Wo, f32).reshape(DC, 128, DC, 128).transpose(0, 3, 2, 1)
    ).reshape(DC, 128, D)
    wh = np.ascontiguousarray(
        np.asarray(L0_ffn_W, f32).reshape(DC, 128, DC, 128).transpose(0, 3, 2, 1)
    ).reshape(DC, 128, D)
    A_all = np.asarray(L0_ffn_A, f32).reshape(128, D)
    ah = np.ascontiguousarray(A_all.reshape(128, DC, 128).transpose(1, 2, 0))
    blt = np.ascontiguousarray(np.asarray(L0_ffn_B, f32).transpose(0, 2, 1).reshape(128, D))
    rwt = np.ascontiguousarray(
        np.asarray(L0_ffn_router_W, f32).T.reshape(DC, 128, 8).transpose(1, 0, 2)
    ).reshape(128, DC * 8)
    rbias = np.ascontiguousarray(np.asarray(L0_ffn_router_b, f32).reshape(1, 8))
    btile = np.ascontiguousarray(np.asarray(L0_ffn_b, f32).reshape(DC, 128).T)
    cosT, sinT = _rope_tables()

    wq = np.asarray(L0_attn_Wq, f32)
    wk = np.asarray(L0_attn_Wk, f32)
    wv = np.asarray(L0_attn_Wv, f32)

    in_maps = []
    for c in range(NCORES):
        wrows = np.concatenate([wq[512 * c:512 * (c + 1)],
                                wk[128 * c:128 * (c + 1)],
                                wv[128 * c:128 * (c + 1)]], axis=0)
        wqkv_h = np.ascontiguousarray(wrows.T).reshape(DC, 128, 768)
        xownT = np.ascontiguousarray(
            np.concatenate([xT[:, 512 * c:512 * c + 512],
                            xT[:, T + 512 * c: T + 512 * c + 512]], axis=1))
        rmask = np.zeros((1, 2 * OWN), f32)
        for blk in range(2):
            for u in range(OWN):
                tpos = 512 * c + u
                if Q_START <= tpos < Q_END:
                    rmask[0, blk * OWN + u] = 1.0 / QMEAN_N
        in_maps.append(dict(
            xT=xT, xownT=xownT, wqkv_h=wqkv_h, woh=woh, wh=wh, ah=ah, blt=blt,
            rwt=rwt, rbias=rbias, btile=btile, rmask=rmask, cosT=cosT, sinT=sinT,
        ))
    return in_maps


_CACHE = {}


def kernel(**inputs):
    in_maps = _host_prep(**inputs)
    if "nc" not in _CACHE:
        _CACHE["nc"] = build_nc(debug=False)
    nc = _CACHE["nc"]
    with allow_ici_all_to_all():
        res = run_bass_kernel_spmd(nc, in_maps, core_ids=list(range(NCORES)))
    out = np.empty((B, T, D), np.float32)
    for c in range(NCORES):
        ot = res.results[c]["out_t"]
        out[0, 512 * c:512 * (c + 1), :] = ot[:, 0:512].T
        out[1, 512 * c:512 * (c + 1), :] = ot[:, 512:1024].T
    return out
